# version 07
# baseline (speedup 1.0000x reference)
import math
import numpy as np
import concourse.bass as bass
import concourse.mybir as mybir
from concourse.bass_utils import run_bass_kernel_spmd
from contextlib import ExitStack

F32 = mybir.dt.float32
BF16 = mybir.dt.bfloat16
AF = mybir.ActivationFunctionType
ALU = mybir.AluOpType
AX = mybir.AxisListType

D = 2048
H = 16
DK = 128
CH = 64
T = 2048
NT128 = 16
NT512 = 4
NCH = 32
EPS = 1e-5
SCALE = DK ** (-0.5)
RANK = 8
FSC = 1.0 / math.sqrt(RANK)

ENGS = ("sync", "scalar", "vector", "tensor", "gpsimd")
DMA_SEMS = ("p0", "p1", "p2", "p3", "p4", "p5", "p6", "p7", "wg", "fa")


class Ctx:

    def __init__(self, nc, sems, eng, e):
        self.nc = nc
        self.sems = sems
        self.cnt = {k: 0 for k in sems}
        self.eng = eng
        self.e = e
        self.last = None
        self.had_compute = set()
        self.pend = {e_: False for e_ in ENGS}

    def inc(self, sem, v):
        self.cnt[sem] += v
        return self.cnt[sem]

    def op(self, eng, fname, *a, inc=None, **kw):
        tgt = self.inc(inc, 1) if inc else None
        self.had_compute.add(eng)
        self.pend[eng] = inc is None
        if self.eng == eng:
            inst = getattr(self.e, fname)(*a, **kw)
            if inc:
                inst.then_inc(self.sems[inc], 1)
            self.last = inst
        return tgt

    def mark(self, eng, sem):
        tgt = self.inc(sem, 1)
        self.pend[eng] = False
        if self.eng == eng:
            self.last.then_inc(self.sems[sem], 1)
        return tgt

    def dma(self, eng, out, in_, sem, **kw):
        tgt = self.inc(sem, 16)
        if self.eng == eng:
            self.e.dma_start(out=out, in_=in_, **kw).then_inc(self.sems[sem], 16)
        return tgt

    def collective(self, ins, outs, sem):
        tgt = self.inc(sem, 1)
        if self.eng == "gpsimd":
            self.e.collective_compute(
                "AllReduce", ALU.add,
                replica_groups=[[0, 1], [2, 3], [4, 5], [6, 7]],
                ins=ins, outs=outs,
            ).then_inc(self.sems[sem])
        return tgt

    def wait(self, eng, sem, val):
        if val and self.eng == eng:
            self.e.wait_ge(self.sems[sem], val)

    def barrier(self, exclude=()):
        for e in ENGS:
            if e in self.had_compute and self.pend[e]:
                self.inc("bar", 1)
                self.pend[e] = False
                if self.eng == e:
                    self.last.then_inc(self.sems["bar"], 1)
        self.had_compute = set()
        tgt = self.cnt["bar"]
        for e in ENGS:
            self.wait(e, "bar", tgt)
            for s in ("pe", "ac", "dv", "gp"):
                self.wait(e, s, self.cnt[s])
            for s in DMA_SEMS:
                if s not in exclude:
                    self.wait(e, s, self.cnt[s])


def build_program(dump=()):
    nc = bass.Bass("TRN2", target_bir_lowering=False, debug=False)

    def dram(name, shape, dtype=F32):
        kind = "ExternalOutput" if name in dump else "Internal"
        return nc.dram_tensor(name, shape, dtype, kind=kind)

    x_tm = nc.dram_tensor("x_tm", [T, D], F32, kind="ExternalInput").ap()
    x_fm = nc.dram_tensor("x_fm", [NT512, 16, 128, 512], F32, kind="ExternalInput").ap()
    wqT = nc.dram_tensor("wqT", [16, 128, D], F32, kind="ExternalInput").ap()
    wkT = nc.dram_tensor("wkT", [16, 128, D], F32, kind="ExternalInput").ap()
    wvT = nc.dram_tensor("wvT", [16, 128, D], F32, kind="ExternalInput").ap()
    woT = nc.dram_tensor("woT", [16, 128, D], F32, kind="ExternalInput").ap()
    fq1T = nc.dram_tensor("fq1T", [128, 128], F32, kind="ExternalInput").ap()
    fq2T = nc.dram_tensor("fq2T", [128, 128], F32, kind="ExternalInput").ap()
    fk1T = nc.dram_tensor("fk1T", [128, 128], F32, kind="ExternalInput").ap()
    fk2T = nc.dram_tensor("fk2T", [128, 128], F32, kind="ExternalInput").ap()
    fq1b = nc.dram_tensor("fq1b", [128], F32, kind="ExternalInput").ap()
    fq2b = nc.dram_tensor("fq2b", [128], F32, kind="ExternalInput").ap()
    fk1b = nc.dram_tensor("fk1b", [128], F32, kind="ExternalInput").ap()
    fk2b = nc.dram_tensor("fk2b", [128], F32, kind="ExternalInput").ap()
    norw = nc.dram_tensor("norw", [128], F32, kind="ExternalInput").ap()
    factsl = nc.dram_tensor("factsl", [RANK, D, 256], F32, kind="ExternalInput").ap()
    fbias = nc.dram_tensor("fbias", [D], F32, kind="ExternalInput").ap()
    maskT = nc.dram_tensor("maskT", [128, 512], F32, kind="ExternalInput").ap()
    ident = nc.dram_tensor("ident", [128, 128], F32, kind="ExternalInput").ap()
    cflag = nc.dram_tensor("cflag", [128, 1], F32, kind="ExternalInput").ap()
    uflag = nc.dram_tensor("uflag", [128, 1], F32, kind="ExternalInput").ap()

    out = nc.dram_tensor("out", [T, D], F32, kind="ExternalOutput").ap()

    mr_sp = dram("mr_sp", [32, 128]).ap()
    xn_sp = dram("xn_sp", [NT512, 16, 128, 512], BF16).ap()
    qm_sp = dram("qm_sp", [NT512, 16, 128, 512], BF16).ap()
    km_sp = dram("km_sp", [NT512, 16, 128, 512], BF16).ap()
    kmt_sp = dram("kmt_sp", [NCH, 64, D], BF16).ap()
    v_sp = dram("v_sp", [NCH, 64, D], BF16).ap()
    o_sp = dram("o_sp", [NT128, 128, 16, 128], BF16).ap()
    cc_f_in = dram("cc_f_in", [128, 4096], BF16)
    cc_f_out = dram("cc_f_out", [1024, 4096], BF16)
    cc_in = dram("cc_in", [128, H * 128])
    cc_out = dram("cc_out", [128, H * 128])
    dbg5 = dram("dbg5", [128, D], BF16).ap()

    A = {}

    def alloc(stack, name, shape, dtype):
        A[name] = stack.enter_context(nc.sbuf_tensor(name, shape, dtype))

    def palloc(stack, name, shape, dtype=F32):
        A[name] = stack.enter_context(nc.psum_tensor(name, shape, dtype))

    persist = ExitStack()
    alloc(persist, "identb", [128, 128], BF16)
    alloc(persist, "ones1f", [1, 128], F32)
    alloc(persist, "ones1b", [1, 128], BF16)
    alloc(persist, "epsb", [128, 1], F32)
    alloc(persist, "b1row", [1, 128], BF16)
    alloc(persist, "b3row", [1, 128], BF16)
    alloc(persist, "ones5b", [1, 512], BF16)
    alloc(persist, "bq2", [128, 1], F32)
    alloc(persist, "bk2", [128, 1], F32)
    alloc(persist, "nw", [128, 1], F32)
    alloc(persist, "flags", [128, 2], F32)
    alloc(persist, "maskb", [128, 512], F32)
    alloc(persist, "brow", [1, D], BF16)

    sS = ExitStack()
    alloc(sS, "xs0", [128, D], F32)
    alloc(sS, "xs1", [128, D], F32)
    alloc(sS, "sqj0", [128, D], F32)
    alloc(sS, "sqj1", [128, D], F32)
    alloc(sS, "scol", [128, 16], F32)
    alloc(sS, "sscol", [128, 16], F32)
    alloc(sS, "mtmp", [128, 16], F32)
    alloc(sS, "vtmp", [128, 16], F32)
    alloc(sS, "mrst", [128, 32], F32)
    alloc(sS, "mrsb", [32, 128], F32)
    alloc(sS, "tmp64", [64, 32], F32)
    sS.close()

    sQ = ExitStack()
    alloc(sQ, "m_row", [1, T], F32)
    alloc(sQ, "r_row", [1, T], F32)
    alloc(sQ, "wq", [128, 16, D], BF16)
    alloc(sQ, "f1", [128, 128], BF16)
    alloc(sQ, "f2", [128, 128], BF16)
    alloc(sQ, "xf", [128, 16, 512], F32)
    alloc(sQ, "xn0", [128, 16, 512], BF16)
    alloc(sQ, "xn1", [128, 16, 512], BF16)
    alloc(sQ, "t1", [128, 512], F32)
    alloc(sQ, "qb0", [128, 512], BF16)
    alloc(sQ, "qb1", [128, 512], BF16)
    alloc(sQ, "q1s0", [128, 512], F32)
    alloc(sQ, "q1s1", [128, 512], F32)
    alloc(sQ, "qm0", [128, 16, 512], BF16)
    alloc(sQ, "qm1", [128, 16, 512], BF16)
    palloc(sQ, "Mb", [128, 512])
    palloc(sQ, "Rb", [128, 512])
    palloc(sQ, "qp0", [128, 512])
    palloc(sQ, "qp1", [128, 512])
    palloc(sQ, "q1p0", [128, 512])
    palloc(sQ, "q1p1", [128, 512])
    palloc(sQ, "q2p0", [128, 512])
    palloc(sQ, "q2p1", [128, 512])
    sQ.close()

    sK = ExitStack()
    alloc(sK, "wk", [128, 16, D], BF16)
    alloc(sK, "g1", [128, 128], BF16)
    alloc(sK, "g2", [128, 128], BF16)
    alloc(sK, "kxn0", [128, 16, 512], BF16)
    alloc(sK, "kxn1", [128, 16, 512], BF16)
    alloc(sK, "kb0", [128, 512], BF16)
    alloc(sK, "kb1", [128, 512], BF16)
    alloc(sK, "k1s0", [128, 512], F32)
    alloc(sK, "k1s1", [128, 512], F32)
    alloc(sK, "km0", [128, 16, 512], BF16)
    alloc(sK, "km1", [128, 16, 512], BF16)
    alloc(sK, "kmt0", [64, D], BF16)
    alloc(sK, "kmt1", [64, D], BF16)
    palloc(sK, "kp0", [128, 512])
    palloc(sK, "kp1", [128, 512])
    palloc(sK, "k1p0", [128, 512])
    palloc(sK, "k1p1", [128, 512])
    palloc(sK, "k2p0", [128, 512])
    palloc(sK, "k2p1", [128, 512])
    palloc(sK, "ktp0", [64, 1024], BF16)
    palloc(sK, "ktp1", [64, 1024], BF16)
    sK.close()

    sV = ExitStack()
    alloc(sV, "wv", [128, 16, D], BF16)
    alloc(sV, "vxn0", [128, 16, 512], BF16)
    alloc(sV, "vxn1", [128, 16, 512], BF16)
    alloc(sV, "vb0", [128, D], BF16)
    alloc(sV, "vb1", [128, D], BF16)
    palloc(sV, "vp0", [128, D])
    palloc(sV, "vp1", [128, D])
    sV.close()

    sTA = ExitStack()
    alloc(sTA, "s0b", [128, H * 128], BF16)

    sT = ExitStack()
    alloc(sT, "facc2", [128, 16, 256], F32)
    alloc(sT, "tkmt0", [64, D], BF16)
    alloc(sT, "tkmt1", [64, D], BF16)
    alloc(sT, "tv0", [64, D], BF16)
    alloc(sT, "tv1", [64, D], BF16)
    alloc(sT, "ttsb", [128, H * 128], F32)
    alloc(sT, "s0f", [128, H * 128], F32)
    palloc(sT, "Tp", [128, H * 128])
    sT.close()

    sA = ExitStack()
    alloc(sA, "aqm0", [128, 16, 512], BF16)
    alloc(sA, "aqm1", [128, 16, 512], BF16)
    alloc(sA, "akm0", [128, 16, 512], BF16)
    alloc(sA, "akm1", [128, 16, 512], BF16)
    alloc(sA, "av0", [128, D], BF16)
    alloc(sA, "av1", [128, D], BF16)
    alloc(sA, "akt0", [128, D], BF16)
    alloc(sA, "akt1", [128, D], BF16)
    alloc(sA, "snap", [128, H * 128], BF16)
    alloc(sA, "sbf0", [128, 2048], BF16)
    alloc(sA, "sbf1", [128, 2048], BF16)
    alloc(sA, "obf0", [128, 512], BF16)
    alloc(sA, "obf1", [128, 512], BF16)
    alloc(sA, "osq", [128, 512], F32)
    alloc(sA, "ss8", [128, 4], F32)
    alloc(sA, "sd8", [128, 4], F32)
    alloc(sA, "rr8", [128, 4], F32)
    alloc(sA, "onb0", [128, 2048], BF16)
    alloc(sA, "onb1", [128, 2048], BF16)
    alloc(sA, "ofm0", [128, 16, 128], BF16)
    alloc(sA, "ofm1", [128, 16, 128], BF16)
    palloc(sA, "St", [128, H * 128])
    palloc(sA, "spp", [128, 512])
    palloc(sA, "op0", [128, 512])
    palloc(sA, "op1", [128, 512])
    palloc(sA, "ofp", [128, 512], BF16)
    sA.close()
    sTA.close()

    sR = ExitStack()
    A["wo"] = sR.enter_context(nc.sbuf_tensor("wo", [128, 16, D], BF16,
                                              side="right"))
    sO = ExitStack()
    alloc(sO, "Fb", [128, 16, D], BF16)
    alloc(sO, "osb0", [128, 16, 128], BF16)
    alloc(sO, "osb1", [128, 16, 128], BF16)
    alloc(sO, "oxn", [128, 16, 512], BF16)
    alloc(sO, "xt0", [128, D], F32)
    alloc(sO, "xt1", [128, D], F32)
    alloc(sO, "ot0", [128, D], F32)
    alloc(sO, "ot1", [128, D], F32)
    alloc(sO, "tf", [128, 512], F32)
    palloc(sO, "ap0", [128, 1024])
    palloc(sO, "ap1", [128, 1024])
    palloc(sO, "fp0", [128, 1024])
    palloc(sO, "fp1", [128, 1024])
    sO.close()
    sR.close()

    sem_stack = ExitStack()
    sems = {}
    for s in DMA_SEMS + ("pe", "ac", "dv", "gp", "cc", "cc2", "bar"):
        sems[s] = sem_stack.enter_context(nc.semaphore(s))

    def emit_program(ctx: Ctx):
        S = lambda n: A[n]

        ld_identb = ctx.dma("gpsimd", S("identb")[:, :], ident, "wg")
        ld_mask = ctx.dma("sync", S("maskb")[:, :], maskT, "p7")
        ctx.dma("sync", S("flags")[:, 0:1], cflag, "p7")
        ld_flags = ctx.dma("sync", S("flags")[:, 1:2], uflag, "p7")
        ld_nw = ctx.dma("sync", S("nw")[:, 0:1], norw[:, None], "p7")
        ld_b2 = ctx.dma("sync", S("bq2")[:, 0:1], fq2b[:, None], "p7")
        ld_b4 = ctx.dma("sync", S("bk2")[:, 0:1], fk2b[:, None], "p7")
        ld_b1 = ctx.dma("gpsimd", S("b1row")[0:1, :], fq1b[None, :], "wg")
        ld_b3 = ctx.dma("gpsimd", S("b3row")[0:1, :], fk1b[None, :], "wg")
        ld_brow = ctx.dma("gpsimd", S("brow")[0:1, :], fbias[None, :], "wg")
        ctx.op("gpsimd", "memset", S("ones1f")[0:1, :], 1.0)
        ctx.op("gpsimd", "memset", S("ones1b")[0:1, :], 1.0)
        ctx.op("gpsimd", "memset", S("ones5b")[0:1, :], 1.0)
        gp_pre = ctx.op("gpsimd", "memset", S("epsb")[:, 0:1], EPS, inc="gp")

        ldx = [ctx.dma("scalar", S("xs0")[:, :], x_tm[0:128, :], "p0"),
               ctx.dma("scalar", S("xs1")[:, :], x_tm[128:256, :], "p1")]
        t_sq = [0, 0]
        t_rd = [0, 0]
        for i in range(NT128):
            par = "p%d" % (i % 2)
            xs = S("xs%d" % (i % 2))
            sq = S("sqj%d" % (i % 2))
            ctx.wait("scalar", par, ldx[i])
            ctx.wait("scalar", "dv", t_rd[i % 2])
            t_sq[i % 2] = ctx.op("scalar", "activation", sq[:, :], xs[:, :],
                                 AF.Square, inc="ac")
            ctx.wait("vector", par, ldx[i])
            ctx.op("vector", "tensor_reduce", S("scol")[:, i:i + 1],
                   xs[:, :], AX.X, ALU.add)
            ctx.wait("vector", "ac", t_sq[i % 2])
            t_rd[i % 2] = ctx.op("vector", "tensor_reduce",
                                 S("sscol")[:, i:i + 1], sq[:, :],
                                 AX.X, ALU.add, inc="dv")
            if i + 2 < NT128:
                ctx.wait("scalar", "dv", t_rd[i % 2])
                ldx.append(ctx.dma("scalar", xs[:, :],
                                   x_tm[128 * (i + 2):128 * (i + 3), :], par))
        ctx.wait("vector", "dv", t_rd[(NT128 - 1) % 2])
        ms = ctx.op("vector", "tensor_scalar_mul", S("mrst")[:, 0:16],
                    S("scol")[:, :], 1.0 / D, inc="dv")
        ctx.wait("vector", "dv", ms)
        ms = ctx.op("vector", "tensor_scalar_mul", S("vtmp")[:, :],
                    S("sscol")[:, :], 1.0 / D, inc="dv")
        ctx.wait("vector", "dv", ms)
        ms = ctx.op("vector", "tensor_tensor", S("mtmp")[:, :],
                    S("mrst")[:, 0:16], S("mrst")[:, 0:16], ALU.mult, inc="dv")
        ctx.wait("vector", "dv", ms)
        dv_var = ctx.op("vector", "tensor_tensor", S("vtmp")[:, :],
                        S("vtmp")[:, :], S("mtmp")[:, :], ALU.subtract,
                        inc="dv")
        ctx.wait("scalar", "dv", dv_var)
        ctx.wait("scalar", "gp", gp_pre)
        ms = ctx.op("scalar", "activation", S("mtmp")[:, :], S("vtmp")[:, :],
                    AF.Ln, bias=S("epsb")[:, 0:1], inc="ac")
        ctx.wait("scalar", "ac", ms)
        ac_std = ctx.op("scalar", "activation", S("mrst")[:, 16:32],
                        S("mtmp")[:, :], AF.Exp, scale=-0.5, inc="ac")
        ctx.wait("vector", "ac", ac_std)
        ms = ctx.op("vector", "tensor_copy", S("tmp64")[0:64, :],
                    S("mrst")[64:128, :], inc="dv")
        ctx.wait("vector", "dv", ms)
        ctx.op("vector", "transpose", S("mrsb")[0:32, 0:32],
               S("mrst")[0:32, 0:32])
        ctx.op("vector", "transpose", S("mrsb")[0:32, 32:64],
               S("mrst")[32:64, 0:32])
        ctx.op("vector", "transpose", S("mrsb")[0:32, 64:96],
               S("tmp64")[0:32, 0:32])
        dv_mr = ctx.op("vector", "transpose", S("mrsb")[0:32, 96:128],
                       S("tmp64")[32:64, 0:32], inc="dv")
        ctx.wait("scalar", "dv", dv_mr)
        ctx.dma("scalar", mr_sp, S("mrsb")[0:32, :], "p2")
        ctx.barrier()

        ld_wq = ctx.dma("gpsimd", S("wq")[:, :, :],
                        wqT.rearrange("c p f -> p c f"), "wg")
        ld_f1 = ctx.dma("gpsimd", S("f1")[:, :], fq1T, "wg")
        ld_f2 = ctx.dma("gpsimd", S("f2")[:, :], fq2T, "wg")
        ctx.dma("sync", S("m_row")[0:1, :],
                mr_sp[0:16, :].rearrange("a b -> (a b)")[None, :], "p5")
        ld_r = ctx.dma("sync", S("r_row")[0:1, :],
                       mr_sp[16:32, :].rearrange("a b -> (a b)")[None, :], "p5")
        ldxf = ctx.dma("sync", S("xf")[:, :, :],
                       x_fm[0].rearrange("c p t -> p c t"), "p0")
        xn_st = [0, 0]
        qm_st = [0, 0]
        pe_tile = [0, 0]
        ac_evac = [0, 0]
        ac_q1 = [0, 0]
        dv_qm = [0, 0]
        for t in range(NT512):
            xn = S("xn%d" % (t % 2))
            qm = S("qm%d" % (t % 2))
            ctx.wait("tensor", "p5", ld_r)
            ctx.wait("tensor", "p0", ldxf)
            pe_mb = ctx.op("tensor", "matmul", A["Mb"][:, :],
                           S("ones1f")[0:1, :],
                           S("m_row")[0:1, 512 * t:512 * (t + 1)],
                           start=True, stop=True)
            pe_rb = ctx.op("tensor", "matmul", A["Rb"][:, :],
                           S("ones1f")[0:1, :],
                           S("r_row")[0:1, 512 * t:512 * (t + 1)],
                           start=True, stop=True, inc="pe")
            ctx.wait("vector", "pe", max(pe_rb, pe_tile[t % 2]))
            ctx.wait("vector", "p%d" % (1 + t % 2), xn_st[t % 2])
            for c in range(16):
                ms = ctx.op("vector", "tensor_tensor", S("t1")[:, :],
                            S("xf")[:, c, :], A["Mb"][:, :], ALU.subtract,
                            inc="dv")
                ctx.wait("vector", "dv", ms)
                dv_xn = ctx.op("vector", "tensor_tensor", xn[:, c, :],
                               S("t1")[:, :], A["Rb"][:, :], ALU.mult,
                               inc="dv" if c == 15 else None)
            ctx.wait("scalar", "dv", dv_xn)
            xn_st[t % 2] = ctx.dma("scalar",
                                   xn_sp[t].rearrange("c p t -> p c t"),
                                   xn[:, :, :], "p%d" % (1 + t % 2))
            if t + 1 < NT512:
                ctx.wait("sync", "dv", dv_xn)
                ldxf = ctx.dma("sync", S("xf")[:, :, :],
                               x_fm[t + 1].rearrange("c p t -> p c t"), "p0")
            ctx.wait("tensor", "wg", max(ld_wq, ld_f1, ld_f2))
            ctx.wait("tensor", "dv", dv_xn)
            ctx.wait("vector", "p7", ld_b2)
            ctx.wait("vector", "p%d" % (3 + t % 2), qm_st[t % 2])
            ac_ev = {}
            pe_q2d = {}
            for h in range(H + 1):
                if h < H:
                    p = h % 2
                    qp = A["qp%d" % p]
                    ctx.wait("tensor", "ac", ac_evac[p])
                    for kc in range(16):
                        ctx.op("tensor", "matmul", qp[:, :],
                               S("wq")[:, kc, 128 * h:128 * (h + 1)],
                               xn[:, kc, :],
                               start=(kc == 0), stop=(kc == 15))
                    pe_q = ctx.mark("tensor", "pe")
                    ctx.wait("scalar", "pe", pe_q)
                    ctx.wait("scalar", "dv", dv_qm[p])
                    ac_evac[p] = ctx.op("scalar", "copy", S("qb%d" % p)[:, :],
                                        qp[:, :], inc="ac")
                    ac_ev[h] = ac_evac[p]
                if h >= 1:
                    hp = h - 1
                    p = hp % 2
                    q1p, q2p = A["q1p%d" % p], A["q2p%d" % p]
                    qb, q1s = S("qb%d" % p), S("q1s%d" % p)
                    ctx.wait("tensor", "ac", max(ac_ev[hp], ac_q1[p]))
                    ctx.wait("tensor", "dv", dv_qm[p])
                    ctx.wait("tensor", "wg", ld_b1)
                    ctx.wait("tensor", "gp", gp_pre)
                    ctx.op("tensor", "matmul", q1p[:, :], S("f1")[:, :],
                           qb[:, :], start=True, stop=False)
                    ctx.op("tensor", "matmul", q1p[:, :], S("b1row")[0:1, :],
                           S("ones5b")[0:1, :], start=False, stop=True)
                    pe_q2 = ctx.op("tensor", "matmul", q2p[:, :], S("f2")[:, :],
                                   qb[:, :], start=True, stop=True, inc="pe")
                    pe_q2d[hp] = pe_q2
                    ctx.wait("scalar", "pe", pe_q2)
                    ac_q1[p] = ctx.op("scalar", "activation", q1s[:, :],
                                      q1p[:, :], AF.Copy, scale=SCALE,
                                      inc="ac")
                    ctx.wait("vector", "ac", ac_q1[p])
                    dv_qm[p] = ctx.op("vector", "scalar_tensor_tensor",
                                      qm[:, hp, :], q2p[:, :], S("bq2")[:, 0:1],
                                      q1s[:, :], ALU.add, ALU.mult, inc="dv")
            pe_tile[t % 2] = pe_q2
            ctx.wait("scalar", "dv", max(dv_qm[0], dv_qm[1]))
            qm_st[t % 2] = ctx.dma("scalar",
                                   qm_sp[t].rearrange("c p t -> p c t"),
                                   qm[:, :, :], "p%d" % (3 + t % 2))
        ctx.barrier()

        ld_wk = ctx.dma("gpsimd", S("wk")[:, :, :],
                        wkT.rearrange("c p f -> p c f"), "wg")
        ld_g1 = ctx.dma("gpsimd", S("g1")[:, :], fk1T, "wg")
        ld_g2 = ctx.dma("gpsimd", S("g2")[:, :], fk2T, "wg")
        ldxn = [ctx.dma("sync", S("kxn0")[:, :, :],
                        xn_sp[0].rearrange("c p t -> p c t"), "p0")]
        km_st = [0, 0]
        kmt_st = [0, 0]
        pe_kt_tile = [0, 0]
        pe_ktile = [0, 0]
        ac_evac = [0, 0]
        ac_k1 = [0, 0]
        dv_km = [0, 0]
        ac_kt = [0, 0]
        for t in range(NT512):
            xn = S("kxn%d" % (t % 2))
            km = S("km%d" % (t % 2))
            if t + 1 < NT512:
                ctx.wait("sync", "pe", pe_ktile[(t + 1) % 2])
                ldxn.append(ctx.dma("sync",
                                    S("kxn%d" % ((t + 1) % 2))[:, :, :],
                                    xn_sp[t + 1].rearrange("c p t -> p c t"),
                                    "p%d" % ((t + 1) % 2)))
            ctx.wait("tensor", "wg", max(ld_wk, ld_g1, ld_g2))
            ctx.wait("tensor", "p%d" % (t % 2), ldxn[t])
            ctx.wait("vector", "p7", ld_b4)
            ctx.wait("vector", "p%d" % (2 + t % 2), km_st[t % 2])
            ctx.wait("vector", "pe", pe_kt_tile[t % 2])
            ac_ev = {}
            for h in range(H + 1):
                if h < H:
                    p = h % 2
                    kp = A["kp%d" % p]
                    ctx.wait("tensor", "ac", ac_evac[p])
                    for kc in range(16):
                        ctx.op("tensor", "matmul", kp[:, :],
                               S("wk")[:, kc, 128 * h:128 * (h + 1)],
                               xn[:, kc, :],
                               start=(kc == 0), stop=(kc == 15))
                    pe_k = ctx.mark("tensor", "pe")
                    ctx.wait("scalar", "pe", pe_k)
                    ctx.wait("scalar", "dv", dv_km[p])
                    ac_evac[p] = ctx.op("scalar", "copy", S("kb%d" % p)[:, :],
                                        kp[:, :], inc="ac")
                    ac_ev[h] = ac_evac[p]
                if h >= 1:
                    hp = h - 1
                    p = hp % 2
                    k1p, k2p = A["k1p%d" % p], A["k2p%d" % p]
                    kb, k1s = S("kb%d" % p), S("k1s%d" % p)
                    ctx.wait("tensor", "ac", max(ac_ev[hp], ac_k1[p]))
                    ctx.wait("tensor", "dv", dv_km[p])
                    ctx.wait("tensor", "wg", ld_b3)
                    ctx.wait("tensor", "gp", gp_pre)
                    ctx.op("tensor", "matmul", k1p[:, :], S("g1")[:, :],
                           kb[:, :], start=True, stop=False)
                    ctx.op("tensor", "matmul", k1p[:, :], S("b3row")[0:1, :],
                           S("ones5b")[0:1, :], start=False, stop=True)
                    pe_k2 = ctx.op("tensor", "matmul", k2p[:, :], S("g2")[:, :],
                                   kb[:, :], start=True, stop=True, inc="pe")
                    ctx.wait("scalar", "pe", pe_k2)
                    ac_k1[p] = ctx.op("scalar", "copy", k1s[:, :], k1p[:, :],
                                      inc="ac")
                    ctx.wait("vector", "ac", ac_k1[p])
                    dv_km[p] = ctx.op("vector", "scalar_tensor_tensor",
                                      km[:, hp, :], k2p[:, :], S("bk2")[:, 0:1],
                                      k1s[:, :], ALU.add, ALU.mult, inc="dv")
            pe_ktile[t % 2] = pe_k2
            ctx.wait("scalar", "dv", max(dv_km[0], dv_km[1]))
            km_st[t % 2] = ctx.dma("scalar",
                                   km_sp[t].rearrange("c p t -> p c t"),
                                   km[:, :, :], "p%d" % (2 + t % 2))
            ctx.wait("tensor", "dv", max(dv_km[0], dv_km[1]))
            ctx.wait("tensor", "wg", ld_identb)
            for cl in range(8):
                kmt = S("kmt%d" % (cl % 2))
                ctx.wait("scalar", "p%d" % (4 + cl % 2), kmt_st[cl % 2])
                for half in range(2):
                    ktp = A["ktp%d" % half]
                    ctx.wait("tensor", "ac", ac_kt[half])
                    for hh in range(8):
                        h = 8 * half + hh
                        ctx.op("tensor", "transpose",
                               ktp[0:64, 128 * hh:128 * (hh + 1)],
                               km[:, h, 64 * cl:64 * (cl + 1)],
                               S("identb")[:, :])
                    pe_t = ctx.mark("tensor", "pe")
                    ctx.wait("scalar", "pe", pe_t)
                    ac_kt[half] = ctx.op(
                        "scalar", "copy",
                        kmt[0:64, 1024 * half:1024 * (half + 1)],
                        ktp[0:64, :], inc="ac")
                ctx.wait("scalar", "ac", ac_kt[1])
                kmt_st[cl % 2] = ctx.dma("scalar", kmt_sp[8 * t + cl],
                                         kmt[0:64, :], "p%d" % (4 + cl % 2))
            pe_kt_tile[t % 2] = pe_t
        ctx.barrier()

        ld_wv = ctx.dma("gpsimd", S("wv")[:, :, :],
                        wvT.rearrange("c p f -> p c f"), "wg")
        ldxn = [ctx.dma("sync", S("vxn0")[:, :, :],
                        xn_sp[0].rearrange("c p t -> p c t"), "p0")]
        v_st = [0, 0]
        pe_vtile = [0, 0]
        ac_v = [0, 0]
        for t in range(NT512):
            xn = S("vxn%d" % (t % 2))
            if t + 1 < NT512:
                ctx.wait("sync", "pe", pe_vtile[(t + 1) % 2])
                ldxn.append(ctx.dma("sync",
                                    S("vxn%d" % ((t + 1) % 2))[:, :, :],
                                    xn_sp[t + 1].rearrange("c p t -> p c t"),
                                    "p%d" % ((t + 1) % 2)))
            ctx.wait("tensor", "p%d" % (t % 2), ldxn[t])
            ctx.wait("tensor", "wg", ld_wv)
            for s in range(4):
                sp = s % 2
                vp = A["vp%d" % sp]
                vb = S("vb%d" % sp)
                ctx.wait("tensor", "ac", ac_v[sp])
                for kc in range(16):
                    for fb in range(4):
                        ctx.op("tensor", "matmul",
                               vp[:, 512 * fb:512 * (fb + 1)],
                               xn[:, kc, 128 * s:128 * (s + 1)],
                               S("wv")[:, kc, 512 * fb:512 * (fb + 1)],
                               start=(kc == 0), stop=(kc == 15))
                pe_v = ctx.mark("tensor", "pe")
                ctx.wait("scalar", "pe", pe_v)
                ctx.wait("scalar", "p%d" % (2 + sp), v_st[sp])
                ac_v[sp] = ctx.op("scalar", "copy", vb[:, :], vp[:, :],
                                  inc="ac")
                ctx.wait("scalar", "ac", ac_v[sp])
                ch = 8 * t + 2 * s
                ctx.dma("scalar", v_sp[ch], vb[0:64, :], "p%d" % (2 + sp))
                v_st[sp] = ctx.dma("scalar", v_sp[ch + 1], vb[64:128, :],
                                   "p%d" % (2 + sp))
            pe_vtile[t % 2] = pe_v
        ctx.barrier()

        ld_kt = [ctx.dma("sync", S("tkmt0")[0:64, :], kmt_sp[0], "p0")]
        ld_tv = [ctx.dma("sync", S("tv0")[0:64, :], v_sp[0], "p0")]
        pe_tc = [0, 0]
        for c in range(NCH):
            kmt = S("tkmt%d" % (c % 2))
            tv = S("tv%d" % (c % 2))
            if c + 1 < NCH:
                par = "p%d" % ((c + 1) % 2)
                ctx.wait("sync", "pe", pe_tc[(c + 1) % 2])
                ld_kt.append(ctx.dma("sync",
                                     S("tkmt%d" % ((c + 1) % 2))[0:64, :],
                                     kmt_sp[c + 1], par))
                ld_tv.append(ctx.dma("sync",
                                     S("tv%d" % ((c + 1) % 2))[0:64, :],
                                     v_sp[c + 1], par))
            ctx.wait("tensor", "p%d" % (c % 2), max(ld_kt[c], ld_tv[c]))
            for h in range(H):
                ctx.op("tensor", "matmul", A["Tp"][:, 128 * h:128 * (h + 1)],
                       kmt[0:64, 128 * h:128 * (h + 1)],
                       tv[0:64, 128 * h:128 * (h + 1)],
                       start=(c == 0 and h % 4 == 0),
                       stop=(c == NCH - 1 and h % 4 == 3))
            pe_tc[c % 2] = ctx.mark("tensor", "pe")
        pe_T = pe_tc[(NCH - 1) % 2]
        ctx.wait("scalar", "pe", pe_T)
        ac_T = ctx.op("scalar", "copy", S("ttsb")[:, :], A["Tp"][:, :],
                      inc="ac")
        ctx.wait("vector", "ac", ac_T)
        ctx.wait("vector", "p7", ld_flags)
        dv_T = ctx.op("vector", "tensor_scalar_mul", S("ttsb")[:, :],
                      S("ttsb")[:, :], S("flags")[:, 0:1], inc="dv")
        ctx.wait("gpsimd", "dv", dv_T)
        g_in = ctx.dma("gpsimd", cc_in[:, :], S("ttsb")[:, :], "wg")
        ctx.wait("gpsimd", "wg", g_in)
        cc_t = ctx.collective([cc_in.ap().opt()], [cc_out.ap().opt()], "cc")
        ctx.wait("gpsimd", "cc", cc_t)
        g_out = ctx.dma("gpsimd", S("s0f")[:, :], cc_out[:, :], "wg")
        ctx.wait("vector", "wg", g_out)
        dv_s0 = ctx.op("vector", "tensor_scalar_mul", S("s0b")[:, :],
                       S("s0f")[:, :], S("flags")[:, 1:2], inc="dv")
        for r in range(RANK):
            kw = {} if r == 0 else {"accum_op": ALU.add}
            fa_t = ctx.dma("gpsimd", S("facc2")[:, :, :],
                           factsl[r].rearrange("(c p) e -> p c e", p=128),
                           "fa", **kw)
            ctx.wait("gpsimd", "fa", fa_t)
        g_fc = ctx.dma("gpsimd", cc_f_in[:, :],
                       S("facc2")[:, :, :].rearrange("p c e -> p (c e)"), "p6")
        ctx.wait("gpsimd", "p6", g_fc)
        ccf = ctx.inc("cc2", 1)
        if ctx.eng == "gpsimd":
            ctx.e.collective_compute(
                "AllGather", ALU.bypass,
                replica_groups=[[0, 1, 2, 3, 4, 5, 6, 7]],
                ins=[cc_f_in.ap().opt()], outs=[cc_f_out.ap().opt()],
            ).then_inc(ctx.sems["cc2"])
        ctx.wait("gpsimd", "cc2", ccf)
        ld_wo = ctx.dma("gpsimd", S("wo")[:, :, :],
                        woT.rearrange("c p f -> p c f"), "wg")
        ctx.barrier(exclude=("fa", "wg"))

        ld_aq = [ctx.dma("sync", S("aqm0")[:, :, :],
                         qm_sp[0].rearrange("c p t -> p c t"), "p2")]
        ld_ak = [ctx.dma("sync", S("akm0")[:, :, :],
                         km_sp[0].rearrange("c p t -> p c t"), "p2")]
        ld_av = [ctx.dma("sync", S("av0")[:, :],
                         v_sp[0:2].rearrange("c p f -> (c p) f"), "p0")]
        ld_akt = [ctx.dma("sync", S("akt0")[:, :],
                          kmt_sp[0:2].rearrange("c p f -> (c p) f"), "p0")]
        ctx.wait("tensor", "dv", dv_s0)
        ctx.wait("tensor", "wg", ld_identb)
        for h in range(H):
            ctx.op("tensor", "matmul", A["St"][:, 128 * h:128 * (h + 1)],
                   S("identb")[:, :], S("s0b")[:, 128 * h:128 * (h + 1)],
                   start=(h % 4 == 0), stop=False)
        pe_init = ctx.mark("tensor", "pe")
        kv_mark = [pe_init] * 4
        dv_mask = {}
        dv_mask_last = 0
        ac_snap_sg = [0, 0, 0, 0]
        ac_oev = [0, 0]
        dv_on_gp = [0, 0]
        dv_on_sg = {}
        pe_of_pair = [0, 0]
        ac_ofm = 0
        st_of = [0, 0]
        NPAIR = NCH // 2

        def scores(jn):
            nonlocal dv_mask_last
            t5n, jln = jn // 4, jn % 4
            qmn = S("aqm%d" % (t5n % 2))
            kmn = S("akm%d" % (t5n % 2))
            tsn = slice(128 * jln, 128 * (jln + 1))
            sbf = S("sbf%d" % (jn % 2))
            for sg in range(4):
                ctx.wait("tensor", "dv", dv_mask_last)
                for hh in range(4):
                    h = 4 * sg + hh
                    ctx.op("tensor", "matmul",
                           A["spp"][:, 128 * hh:128 * (hh + 1)],
                           kmn[:, h, tsn], qmn[:, h, tsn],
                           start=(hh == 0), stop=(hh == 3))
                pe_sp = ctx.mark("tensor", "pe")
                ctx.wait("vector", "pe", pe_sp)
                ctx.wait("vector", "p7", ld_mask)
                dv_mask[(jn, sg)] = ctx.op(
                    "vector", "tensor_tensor",
                    sbf[:, 512 * sg:512 * (sg + 1)],
                    A["spp"][:, :], S("maskb")[:, :], ALU.mult, inc="dv")
                dv_mask_last = dv_mask[(jn, sg)]

        def transposes(jp):
            nonlocal ac_ofm, pe_of_pair, st_of
            par = jp % 2
            onb = S("onb%d" % par)
            ofm = S("ofm%d" % par)
            for sg in range(4):
                ctx.wait("tensor", "dv", dv_on_sg[(jp, sg)])
                ctx.wait("tensor", "ac", ac_ofm)
                for hh in range(4):
                    ctx.op("tensor", "transpose",
                           A["ofp"][:, 128 * hh:128 * (hh + 1)],
                           onb[:, 512 * sg + 128 * hh:512 * sg + 128 * hh + 128],
                           S("identb")[:, :])
                pe_of = ctx.mark("tensor", "pe")
                ctx.wait("scalar", "pe", pe_of)
                ctx.wait("scalar", "p%d" % (4 + par), st_of[par])
                ac_ofm = ctx.op("scalar", "copy",
                                ofm[:, 4 * sg:4 * (sg + 1), :],
                                A["ofp"][:, :]
                                .rearrange("p (h t) -> p h t", h=4),
                                inc="ac")
            pe_of_pair[par] = pe_of
            ctx.wait("scalar", "ac", ac_ofm)
            st_of[par] = ctx.dma("scalar", o_sp[jp], ofm[:, :, :],
                                 "p%d" % (4 + par))

        ctx.wait("tensor", "p2", ld_ak[0])
        scores(0)
        for j in range(NPAIR):
            t5, jl = j // 4, j % 4
            qm = S("aqm%d" % (t5 % 2))
            akt = S("akt%d" % (j % 2))
            av = S("av%d" % (j % 2))
            if jl == 0 and t5 + 1 < NT512:
                par = "p%d" % (2 + (t5 + 1) % 2)
                ctx.wait("sync", "pe", kv_mark[3])
                ld_aq.append(ctx.dma("sync",
                                     S("aqm%d" % ((t5 + 1) % 2))[:, :, :],
                                     qm_sp[t5 + 1].rearrange("c p t -> p c t"),
                                     par))
                ld_ak.append(ctx.dma("sync",
                                     S("akm%d" % ((t5 + 1) % 2))[:, :, :],
                                     km_sp[t5 + 1].rearrange("c p t -> p c t"),
                                     par))
            if j + 1 < NPAIR:
                par = "p%d" % ((j + 1) % 2)
                ctx.wait("sync", "pe", kv_mark[3])
                ld_akt.append(ctx.dma(
                    "sync", S("akt%d" % ((j + 1) % 2))[:, :],
                    kmt_sp[2 * j + 2:2 * j + 4].rearrange("c p f -> (c p) f"),
                    par))
                ld_av.append(ctx.dma(
                    "sync", S("av%d" % ((j + 1) % 2))[:, :],
                    v_sp[2 * j + 2:2 * j + 4].rearrange("c p f -> (c p) f"),
                    par))
            tsl = slice(128 * jl, 128 * (jl + 1))
            ctx.wait("tensor", "p%d" % (2 + t5 % 2), ld_ak[t5])
            ctx.wait("tensor", "p%d" % (j % 2), max(ld_av[j], ld_akt[j]))
            sbf = S("sbf%d" % (j % 2))
            for sg in range(4):
                ctx.wait("scalar", "pe", kv_mark[sg])
                ac_snap_sg[sg] = ctx.op(
                    "scalar", "copy", S("snap")[:, 512 * sg:512 * (sg + 1)],
                    A["St"][:, 512 * sg:512 * (sg + 1)], inc="ac")
            for sg in range(4):
                gp_ = (4 * j + sg) % 2
                op = A["op%d" % gp_]
                obf = S("obf%d" % gp_)
                ctx.wait("tensor", "ac", max(ac_snap_sg[sg], ac_oev[gp_]))
                for hh in range(4):
                    h = 4 * sg + hh
                    ctx.op("tensor", "matmul",
                           op[:, 128 * hh:128 * (hh + 1)],
                           qm[:, h, tsl],
                           S("snap")[:, 128 * h:128 * (h + 1)],
                           start=(hh == 0), stop=False)
                ctx.wait("tensor", "dv", dv_mask[(j, sg)])
                for hh in range(4):
                    h = 4 * sg + hh
                    ctx.op("tensor", "matmul",
                           op[:, 128 * hh:128 * (hh + 1)],
                           sbf[:, 512 * sg + 128 * hh:512 * sg + 128 * hh + 128],
                           av[:, 128 * h:128 * (h + 1)],
                           start=False, stop=(hh == 3))
                pe_o = ctx.mark("tensor", "pe")
                for hh in range(4):
                    h = 4 * sg + hh
                    ctx.op("tensor", "matmul",
                           A["St"][:, 128 * h:128 * (h + 1)],
                           akt[:, 128 * h:128 * (h + 1)],
                           av[:, 128 * h:128 * (h + 1)],
                           start=False,
                           stop=(j == NPAIR - 1 and hh == 3))
                kv_mark[sg] = ctx.mark("tensor", "pe")
                ctx.wait("scalar", "pe", pe_o)
                ctx.wait("scalar", "dv", dv_on_gp[gp_])
                ac_oev[gp_] = ctx.op("scalar", "copy", obf[:, :], op[:, :],
                                     inc="ac")
                ctx.wait("vector", "ac", ac_oev[gp_])
                ms = ctx.op("vector", "tensor_tensor", S("osq")[:, :],
                            obf[:, :], obf[:, :], ALU.mult, inc="dv")
                ctx.wait("vector", "dv", ms)
                ctx.op("vector", "tensor_reduce",
                       S("ss8")[:, :].unsqueeze(2),
                       S("osq")[:, :].rearrange("p (g e) -> p g e", g=4),
                       AX.X, ALU.add)
                dv_ss = ctx.mark("vector", "dv")
                ctx.wait("scalar", "dv", dv_ss)
                ms = ctx.op("scalar", "activation", S("sd8")[:, :],
                            S("ss8")[:, :], AF.Ln, scale=1.0 / DK,
                            bias=S("epsb")[:, 0:1], inc="ac")
                ctx.wait("scalar", "ac", ms)
                ac_sd = ctx.op("scalar", "activation", S("rr8")[:, :],
                               S("sd8")[:, :], AF.Exp, scale=-0.5, inc="ac")
                ctx.wait("vector", "ac", ac_sd)
                ctx.wait("vector", "pe", pe_of_pair[j % 2])
                dv_on = ctx.op("vector", "tensor_tensor",
                               S("onb%d" % (j % 2))
                               [:, 512 * sg:512 * (sg + 1)]
                               .rearrange("p (g e) -> p g e", g=4),
                               obf[:, :].rearrange("p (g e) -> p g e", g=4),
                               S("rr8")[:, :].unsqueeze(2)
                               .broadcast_to([128, 4, DK]),
                               ALU.mult, inc="dv")
                dv_on_gp[gp_] = dv_on
                dv_on_sg[(j, sg)] = dv_on
            if j + 1 < NPAIR:
                ctx.wait("tensor", "p%d" % (2 + ((j + 1) // 4) % 2),
                         ld_ak[(j + 1) // 4])
                scores(j + 1)
            if j >= 1:
                transposes(j - 1)
        transposes(NPAIR - 1)
        ctx.wait("vector", "wg", ld_wo)
        ctx.wait("vector", "p7", ld_nw)
        dv_wo = ctx.op("vector", "tensor_scalar_mul",
                       S("wo")[:, :, :].rearrange("p c f -> p (c f)"),
                       S("wo")[:, :, :].rearrange("p c f -> p (c f)"),
                       S("nw")[:, 0:1], inc="dv")
        ctx.wait("scalar", "pe", max(kv_mark))
        ms = ctx.op("scalar", "copy", S("snap")[:, :], A["St"][:, :], inc="ac")
        ctx.wait("scalar", "ac", ms)
        ctx.dma("scalar", dbg5, S("snap")[:, :], "p6")
        ctx.barrier(exclude=("fa",))

        ccv = cc_f_out.ap().rearrange("(r p) (c e) -> r p c e", r=8, c=16)
        for r in range(8):
            ld_F = ctx.dma("sync", S("Fb")[:, :, 256 * r:256 * (r + 1)],
                           ccv[r], "p7")
        ld_o = [ctx.dma("sync", S("osb0")[:, :, :], o_sp[0], "p0")]
        ld_xt = [ctx.dma("scalar", S("xt0")[:, :], x_tm[0:128, :], "p2")]
        ld_ox = [ctx.dma("sync", S("oxn")[:, :, :],
                         xn_sp[0].rearrange("c p t -> p c t"), "p6")]
        dv_out = [0, 0]
        st_out = [0, 0]
        dv_psum = [0, 0]
        pe_otile = [0, 0]
        for t in range(NT128):
            t5, s = t // 4, t % 4
            osb = S("osb%d" % (t % 2))
            xt = S("xt%d" % (t % 2))
            if t + 1 < NT128:
                ctx.wait("sync", "pe", pe_otile[(t + 1) % 2])
                ld_o.append(ctx.dma("sync",
                                    S("osb%d" % ((t + 1) % 2))[:, :, :],
                                    o_sp[t + 1], "p%d" % ((t + 1) % 2)))
                ctx.wait("scalar", "dv", dv_out[(t + 1) % 2])
                ld_xt.append(ctx.dma("scalar",
                                     S("xt%d" % ((t + 1) % 2))[:, :],
                                     x_tm[128 * (t + 1):128 * (t + 2), :],
                                     "p%d" % (2 + (t + 1) % 2)))
            ctx.wait("tensor", "p%d" % (t % 2), ld_o[t])
            ctx.wait("tensor", "p6", ld_ox[t5])
            ctx.wait("tensor", "p7", ld_F)
            ctx.wait("tensor", "dv", dv_wo)
            ctx.wait("tensor", "wg", ld_wo)
            ctx.wait("vector", "p%d" % (2 + t % 2), ld_xt[t])
            ctx.wait("vector", "p%d" % (4 + t % 2), st_out[t % 2])
            for fbp in range(2):
                pp = (2 * t + fbp) % 2
                ap, fp = A["ap%d" % pp], A["fp%d" % pp]
                ctx.wait("tensor", "dv", dv_psum[pp])
                for fb2 in range(2):
                    fb = 2 * fbp + fb2
                    fsl = slice(512 * fb, 512 * (fb + 1))
                    psl = slice(512 * fb2, 512 * (fb2 + 1))
                    for h in range(H):
                        ctx.op("tensor", "matmul", ap[:, psl],
                               osb[:, h, :], S("wo")[:, h, fsl],
                               start=(h == 0), stop=False)
                    ctx.op("tensor", "matmul", ap[:, psl],
                           S("ones1b")[0:1, :], S("brow")[0:1, fsl],
                           start=False, stop=True)
                    for kc in range(16):
                        ctx.op("tensor", "matmul", fp[:, psl],
                               S("oxn")[:, kc, 128 * s:128 * (s + 1)],
                               S("Fb")[:, kc, fsl],
                               start=(kc == 0), stop=(kc == 15))
                pe_fb = ctx.mark("tensor", "pe")
                ctx.wait("vector", "pe", pe_fb)
                for fb2 in range(2):
                    fb = 2 * fbp + fb2
                    fsl = slice(512 * fb, 512 * (fb + 1))
                    psl = slice(512 * fb2, 512 * (fb2 + 1))
                    ms = ctx.op("vector", "scalar_tensor_tensor", S("tf")[:, :],
                                fp[:, psl], FSC, xt[:, fsl],
                                ALU.mult, ALU.add, inc="dv")
                    ctx.wait("vector", "dv", ms)
                    dv_last = ctx.op("vector", "tensor_tensor",
                                     S("ot%d" % (t % 2))[:, fsl],
                                     S("tf")[:, :], ap[:, psl], ALU.add,
                                     inc="dv")
                dv_psum[pp] = dv_last
            pe_otile[t % 2] = pe_fb
            dv_out[t % 2] = dv_last
            if s == 3 and t5 + 1 < NT512:
                ctx.wait("sync", "pe", pe_fb)
                ld_ox.append(ctx.dma("sync", S("oxn")[:, :, :],
                                     xn_sp[t5 + 1].rearrange("c p t -> p c t"),
                                     "p6"))
            ctx.wait("scalar", "dv", dv_out[t % 2])
            st_out[t % 2] = ctx.dma("scalar", out[128 * t:128 * (t + 1), :],
                                    S("ot%d" % (t % 2))[:, :],
                                    "p%d" % (4 + t % 2))
        ctx.barrier()

    with nc.Block() as block:
        for eng in ENGS:
            def body(e, eng=eng):
                emit_program(Ctx(nc, sems, eng, e))
            getattr(block, eng)(body)
    sem_stack.close()
    persist.close()
    return nc


def _layout_inputs(inputs):
    x = np.ascontiguousarray(np.asarray(inputs["L0_attn_x"], dtype=np.float32))
    xr = x.reshape(16384, D)
    shared = {
        "wqT": np.ascontiguousarray(np.asarray(inputs["L0_attn_wq"]).T).reshape(16, 128, D),
        "wkT": np.ascontiguousarray(np.asarray(inputs["L0_attn_wk"]).T).reshape(16, 128, D),
        "wvT": np.ascontiguousarray(np.asarray(inputs["L0_attn_wv"]).T).reshape(16, 128, D),
        "woT": np.ascontiguousarray(np.asarray(inputs["L0_attn_wo"]).T).reshape(16, 128, D),
        "fq1T": np.ascontiguousarray(np.asarray(inputs["L0_attn_fq1_w"]).T),
        "fq2T": np.ascontiguousarray(np.asarray(inputs["L0_attn_fq2_w"]).T),
        "fk1T": np.ascontiguousarray(np.asarray(inputs["L0_attn_fk1_w"]).T),
        "fk2T": np.ascontiguousarray(np.asarray(inputs["L0_attn_fk2_w"]).T),
        "fq1b": np.asarray(inputs["L0_attn_fq1_b"], dtype=np.float32),
        "fq2b": np.asarray(inputs["L0_attn_fq2_b"], dtype=np.float32),
        "fk1b": np.asarray(inputs["L0_attn_fk1_b"], dtype=np.float32),
        "fk2b": np.asarray(inputs["L0_attn_fk2_b"], dtype=np.float32),
        "norw": np.asarray(inputs["L0_attn_norm_w"], dtype=np.float32),

        "fbias": np.asarray(inputs["L0_ffn_bias"], dtype=np.float32),
        "maskT": np.tile(np.block(
            [[np.triu(np.ones((64, 64), np.float32)), np.ones((64, 64), np.float32)],
             [np.zeros((64, 64), np.float32), np.triu(np.ones((64, 64), np.float32))]]),
            (1, 4)),
        "ident": np.eye(128, dtype=np.float32),
    }
    fact = np.asarray(inputs["L0_ffn_factors"], dtype=np.float32)
    in_maps = []
    for c in range(8):
        x_tm = np.ascontiguousarray(xr[2048 * c:2048 * (c + 1)])
        m = dict(shared)
        m["factsl"] = np.ascontiguousarray(fact[:, :, 256 * c:256 * (c + 1)])
        m["x_tm"] = x_tm
        m["x_fm"] = np.ascontiguousarray(
            x_tm.reshape(4, 512, 16, 128).transpose(0, 2, 3, 1))
        m["cflag"] = np.full((128, 1), 1.0 if c % 2 == 0 else 0.0, np.float32)
        m["uflag"] = np.full((128, 1), 0.0 if c % 2 == 0 else 1.0, np.float32)
        in_maps.append(m)
    return in_maps


def kernel(**inputs):
    in_maps = _layout_inputs(inputs)
    nc = build_program()
    res = run_bass_kernel_spmd(nc, in_maps, core_ids=list(range(8)))
    out = np.empty((4, 4096, D), np.float32)
    for c in range(8):
        out[c // 2, 2048 * (c % 2):2048 * (c % 2 + 1), :] = res.results[c]["out"]
    return out


# version 08
# speedup vs baseline: 1.0572x; 1.0572x over previous
import math
import numpy as np
import concourse.bass as bass
import concourse.mybir as mybir
from concourse.bass_utils import run_bass_kernel_spmd
from contextlib import ExitStack

F32 = mybir.dt.float32
BF16 = mybir.dt.bfloat16
AF = mybir.ActivationFunctionType
ALU = mybir.AluOpType
AX = mybir.AxisListType

D = 2048
H = 16
DK = 128
CH = 64
T = 2048
NT128 = 16
NT512 = 4
NCH = 32
EPS = 1e-5
SCALE = DK ** (-0.5)
RANK = 8
FSC = 1.0 / math.sqrt(RANK)

ENGS = ("sync", "scalar", "vector", "tensor", "gpsimd")
DMA_SEMS = ("p0", "p1", "p2", "p3", "p4", "p5", "p6", "p7", "wg", "fa")


class Ctx:

    def __init__(self, nc, sems, eng, e):
        self.nc = nc
        self.sems = sems
        self.cnt = {k: 0 for k in sems}
        self.eng = eng
        self.e = e
        self.last = None
        self.had_compute = set()
        self.pend = {e_: False for e_ in ENGS}

    def inc(self, sem, v):
        self.cnt[sem] += v
        return self.cnt[sem]

    def op(self, eng, fname, *a, inc=None, **kw):
        tgt = self.inc(inc, 1) if inc else None
        self.had_compute.add(eng)
        self.pend[eng] = inc is None
        if self.eng == eng:
            inst = getattr(self.e, fname)(*a, **kw)
            if inc:
                inst.then_inc(self.sems[inc], 1)
            self.last = inst
        return tgt

    def mark(self, eng, sem):
        tgt = self.inc(sem, 1)
        self.pend[eng] = False
        if self.eng == eng:
            self.last.then_inc(self.sems[sem], 1)
        return tgt

    def dma(self, eng, out, in_, sem, **kw):
        tgt = self.inc(sem, 16)
        if self.eng == eng:
            self.e.dma_start(out=out, in_=in_, **kw).then_inc(self.sems[sem], 16)
        return tgt

    def collective(self, ins, outs, sem):
        tgt = self.inc(sem, 1)
        if self.eng == "gpsimd":
            self.e.collective_compute(
                "AllReduce", ALU.add,
                replica_groups=[[0, 1], [2, 3], [4, 5], [6, 7]],
                ins=ins, outs=outs,
            ).then_inc(self.sems[sem])
        return tgt

    def wait(self, eng, sem, val):
        if val and self.eng == eng:
            self.e.wait_ge(self.sems[sem], val)

    def barrier(self, exclude=()):
        for e in ENGS:
            if e in self.had_compute and self.pend[e]:
                self.inc("bar", 1)
                self.pend[e] = False
                if self.eng == e:
                    self.last.then_inc(self.sems["bar"], 1)
        self.had_compute = set()
        tgt = self.cnt["bar"]
        for e in ENGS:
            self.wait(e, "bar", tgt)
            for s in ("pe", "ac", "dv", "gp"):
                self.wait(e, s, self.cnt[s])
            for s in DMA_SEMS:
                if s not in exclude:
                    self.wait(e, s, self.cnt[s])


def build_program(dump=()):
    nc = bass.Bass("TRN2", target_bir_lowering=False, debug=False)

    def dram(name, shape, dtype=F32):
        kind = "ExternalOutput" if name in dump else "Internal"
        return nc.dram_tensor(name, shape, dtype, kind=kind)

    x_tm = nc.dram_tensor("x_tm", [T, D], F32, kind="ExternalInput").ap()
    x_fm = nc.dram_tensor("x_fm", [NT512, 16, 128, 512], F32, kind="ExternalInput").ap()
    wqT = nc.dram_tensor("wqT", [16, 128, D], F32, kind="ExternalInput").ap()
    wkT = nc.dram_tensor("wkT", [16, 128, D], F32, kind="ExternalInput").ap()
    wvT = nc.dram_tensor("wvT", [16, 128, D], F32, kind="ExternalInput").ap()
    woT = nc.dram_tensor("woT", [16, 128, D], F32, kind="ExternalInput").ap()
    fq1T = nc.dram_tensor("fq1T", [128, 128], F32, kind="ExternalInput").ap()
    fq2T = nc.dram_tensor("fq2T", [128, 128], F32, kind="ExternalInput").ap()
    fk1T = nc.dram_tensor("fk1T", [128, 128], F32, kind="ExternalInput").ap()
    fk2T = nc.dram_tensor("fk2T", [128, 128], F32, kind="ExternalInput").ap()
    fq1b = nc.dram_tensor("fq1b", [128], F32, kind="ExternalInput").ap()
    fq2b = nc.dram_tensor("fq2b", [128], F32, kind="ExternalInput").ap()
    fk1b = nc.dram_tensor("fk1b", [128], F32, kind="ExternalInput").ap()
    fk2b = nc.dram_tensor("fk2b", [128], F32, kind="ExternalInput").ap()
    norw = nc.dram_tensor("norw", [128], F32, kind="ExternalInput").ap()
    factsl = nc.dram_tensor("factsl", [RANK, D, 256], F32, kind="ExternalInput").ap()
    fbias = nc.dram_tensor("fbias", [D], F32, kind="ExternalInput").ap()
    maskT = nc.dram_tensor("maskT", [128, 512], F32, kind="ExternalInput").ap()
    ident = nc.dram_tensor("ident", [128, 128], F32, kind="ExternalInput").ap()
    cflag = nc.dram_tensor("cflag", [128, 1], F32, kind="ExternalInput").ap()
    uflag = nc.dram_tensor("uflag", [128, 1], F32, kind="ExternalInput").ap()

    out = nc.dram_tensor("out", [T, D], F32, kind="ExternalOutput").ap()

    mr_sp = dram("mr_sp", [32, 128]).ap()
    xn_sp = dram("xn_sp", [NT512, 16, 128, 512], BF16).ap()
    qm_sp = dram("qm_sp", [NT512, 16, 128, 512], BF16).ap()
    km_sp = dram("km_sp", [NT512, 16, 128, 512], BF16).ap()
    kmt_sp = dram("kmt_sp", [NCH, 64, D], BF16).ap()
    v_sp = dram("v_sp", [NCH, 64, D], BF16).ap()
    o_sp = dram("o_sp", [NT128, 128, 16, 128], BF16).ap()
    cc_f_in = dram("cc_f_in", [128, 4096], BF16)
    cc_f_out = dram("cc_f_out", [1024, 4096], BF16)
    cc_in = dram("cc_in", [128, H * 128])
    cc_out = dram("cc_out", [128, H * 128])
    dbg5 = dram("dbg5", [128, D], BF16).ap()

    A = {}

    def alloc(stack, name, shape, dtype):
        A[name] = stack.enter_context(nc.sbuf_tensor(name, shape, dtype))

    def palloc(stack, name, shape, dtype=F32):
        A[name] = stack.enter_context(nc.psum_tensor(name, shape, dtype))

    persist = ExitStack()
    alloc(persist, "identb", [128, 128], BF16)
    alloc(persist, "ones1f", [1, 128], F32)
    alloc(persist, "ones1b", [1, 128], BF16)
    alloc(persist, "epsb", [128, 1], F32)
    alloc(persist, "b1row", [1, 128], BF16)
    alloc(persist, "b3row", [1, 128], BF16)
    alloc(persist, "ones5b", [1, 512], BF16)
    alloc(persist, "bq2", [128, 1], F32)
    alloc(persist, "bk2", [128, 1], F32)
    alloc(persist, "nw", [128, 1], F32)
    alloc(persist, "flags", [128, 2], F32)
    alloc(persist, "maskb", [128, 512], F32)
    alloc(persist, "brow", [1, D], BF16)

    sS = ExitStack()
    alloc(sS, "xs0", [128, D], F32)
    alloc(sS, "xs1", [128, D], F32)
    alloc(sS, "sqj0", [128, D], F32)
    alloc(sS, "sqj1", [128, D], F32)
    alloc(sS, "scol", [128, 16], F32)
    alloc(sS, "sscol", [128, 16], F32)
    alloc(sS, "mtmp", [128, 16], F32)
    alloc(sS, "vtmp", [128, 16], F32)
    alloc(sS, "mrst", [128, 32], F32)
    alloc(sS, "mrsb", [32, 128], F32)
    alloc(sS, "tmp64", [64, 32], F32)
    sS.close()

    sQ = ExitStack()
    alloc(sQ, "m_row", [1, T], F32)
    alloc(sQ, "r_row", [1, T], F32)
    alloc(sQ, "wq", [128, 16, D], BF16)
    alloc(sQ, "f1", [128, 128], BF16)
    alloc(sQ, "f2", [128, 128], BF16)
    alloc(sQ, "xf", [128, 16, 512], F32)
    alloc(sQ, "xn0", [128, 16, 512], BF16)
    alloc(sQ, "xn1", [128, 16, 512], BF16)
    alloc(sQ, "t1", [128, 512], F32)
    alloc(sQ, "qb0", [128, 512], BF16)
    alloc(sQ, "qb1", [128, 512], BF16)
    alloc(sQ, "q1s0", [128, 512], F32)
    alloc(sQ, "q1s1", [128, 512], F32)
    alloc(sQ, "qm0", [128, 16, 512], BF16)
    alloc(sQ, "qm1", [128, 16, 512], BF16)
    palloc(sQ, "Mb", [128, 512])
    palloc(sQ, "Rb", [128, 512])
    palloc(sQ, "qp0", [128, 512])
    palloc(sQ, "qp1", [128, 512])
    palloc(sQ, "q1p0", [128, 512])
    palloc(sQ, "q1p1", [128, 512])
    palloc(sQ, "q2p0", [128, 512])
    palloc(sQ, "q2p1", [128, 512])
    sQ.close()

    sK = ExitStack()
    alloc(sK, "wk", [128, 16, D], BF16)
    alloc(sK, "g1", [128, 128], BF16)
    alloc(sK, "g2", [128, 128], BF16)
    alloc(sK, "kxn0", [128, 16, 512], BF16)
    alloc(sK, "kxn1", [128, 16, 512], BF16)
    alloc(sK, "kb0", [128, 512], BF16)
    alloc(sK, "kb1", [128, 512], BF16)
    alloc(sK, "k1s0", [128, 512], F32)
    alloc(sK, "k1s1", [128, 512], F32)
    alloc(sK, "km0", [128, 16, 512], BF16)
    alloc(sK, "km1", [128, 16, 512], BF16)
    alloc(sK, "kmt0", [64, D], BF16)
    alloc(sK, "kmt1", [64, D], BF16)
    palloc(sK, "kp0", [128, 512])
    palloc(sK, "kp1", [128, 512])
    palloc(sK, "k1p0", [128, 512])
    palloc(sK, "k1p1", [128, 512])
    palloc(sK, "k2p0", [128, 512])
    palloc(sK, "k2p1", [128, 512])
    palloc(sK, "ktp0", [64, 1024], BF16)
    palloc(sK, "ktp1", [64, 1024], BF16)
    sK.close()

    sV = ExitStack()
    alloc(sV, "wv", [128, 16, D], BF16)
    alloc(sV, "vxn0", [128, 16, 512], BF16)
    alloc(sV, "vxn1", [128, 16, 512], BF16)
    alloc(sV, "vb0", [128, D], BF16)
    alloc(sV, "vb1", [128, D], BF16)
    palloc(sV, "vp0", [128, D])
    palloc(sV, "vp1", [128, D])
    sV.close()

    sTA = ExitStack()
    alloc(sTA, "s0b", [128, H * 128], BF16)
    alloc(sTA, "facc2", [128, 16, 256], F32)

    sT = ExitStack()
    alloc(sT, "tkmt0", [64, D], BF16)
    alloc(sT, "tkmt1", [64, D], BF16)
    alloc(sT, "tv0", [64, D], BF16)
    alloc(sT, "tv1", [64, D], BF16)
    alloc(sT, "ttsb", [128, H * 128], F32)
    alloc(sT, "s0f", [128, H * 128], F32)
    palloc(sT, "Tp", [128, H * 128])
    sT.close()

    sA = ExitStack()
    alloc(sA, "aqm0", [128, 16, 512], BF16)
    alloc(sA, "aqm1", [128, 16, 512], BF16)
    alloc(sA, "akm0", [128, 16, 512], BF16)
    alloc(sA, "akm1", [128, 16, 512], BF16)
    alloc(sA, "av0", [128, D], BF16)
    alloc(sA, "av1", [128, D], BF16)
    alloc(sA, "akt0", [128, D], BF16)
    alloc(sA, "akt1", [128, D], BF16)
    alloc(sA, "snap", [128, H * 128], BF16)
    alloc(sA, "sbf0", [128, 2048], BF16)
    alloc(sA, "sbf1", [128, 2048], BF16)
    alloc(sA, "obf0", [128, 512], BF16)
    alloc(sA, "obf1", [128, 512], BF16)
    alloc(sA, "osq", [128, 512], F32)
    alloc(sA, "ss8", [128, 4], F32)
    alloc(sA, "sd8", [128, 4], F32)
    alloc(sA, "rr8", [128, 4], F32)
    alloc(sA, "onb0", [128, 2048], BF16)
    alloc(sA, "onb1", [128, 2048], BF16)
    alloc(sA, "ofm0", [128, 16, 128], BF16)
    alloc(sA, "ofm1", [128, 16, 128], BF16)
    palloc(sA, "St", [128, H * 128])
    palloc(sA, "spp", [128, 512])
    palloc(sA, "op0", [128, 512])
    palloc(sA, "op1", [128, 512])
    palloc(sA, "ofp", [128, 512], BF16)
    sA.close()
    sTA.close()

    sR = ExitStack()
    A["wo"] = sR.enter_context(nc.sbuf_tensor("wo", [128, 16, D], BF16,
                                              side="right"))
    sO = ExitStack()
    alloc(sO, "Fb", [128, 16, D], BF16)
    alloc(sO, "osb0", [128, 16, 128], BF16)
    alloc(sO, "osb1", [128, 16, 128], BF16)
    alloc(sO, "oxn", [128, 16, 512], BF16)
    alloc(sO, "xt0", [128, D], F32)
    alloc(sO, "xt1", [128, D], F32)
    alloc(sO, "ot0", [128, D], F32)
    alloc(sO, "ot1", [128, D], F32)
    alloc(sO, "tf", [128, 512], F32)
    palloc(sO, "ap0", [128, 1024])
    palloc(sO, "ap1", [128, 1024])
    palloc(sO, "fp0", [128, 1024])
    palloc(sO, "fp1", [128, 1024])
    sO.close()
    sR.close()

    sem_stack = ExitStack()
    sems = {}
    for s in DMA_SEMS + ("pe", "ac", "dv", "gp", "cc", "cc2", "bar"):
        sems[s] = sem_stack.enter_context(nc.semaphore(s))

    def emit_program(ctx: Ctx):
        S = lambda n: A[n]

        ld_identb = ctx.dma("gpsimd", S("identb")[:, :], ident, "wg")
        ld_mask = ctx.dma("sync", S("maskb")[:, :], maskT, "p7")
        ctx.dma("sync", S("flags")[:, 0:1], cflag, "p7")
        ld_flags = ctx.dma("sync", S("flags")[:, 1:2], uflag, "p7")
        ld_nw = ctx.dma("sync", S("nw")[:, 0:1], norw[:, None], "p7")
        ld_b2 = ctx.dma("sync", S("bq2")[:, 0:1], fq2b[:, None], "p7")
        ld_b4 = ctx.dma("sync", S("bk2")[:, 0:1], fk2b[:, None], "p7")
        ld_b1 = ctx.dma("gpsimd", S("b1row")[0:1, :], fq1b[None, :], "wg")
        ld_b3 = ctx.dma("gpsimd", S("b3row")[0:1, :], fk1b[None, :], "wg")
        ld_brow = ctx.dma("gpsimd", S("brow")[0:1, :], fbias[None, :], "wg")
        ctx.op("gpsimd", "memset", S("ones1f")[0:1, :], 1.0)
        ctx.op("gpsimd", "memset", S("ones1b")[0:1, :], 1.0)
        ctx.op("gpsimd", "memset", S("ones5b")[0:1, :], 1.0)
        gp_pre = ctx.op("gpsimd", "memset", S("epsb")[:, 0:1], EPS, inc="gp")

        ldx = [ctx.dma("scalar", S("xs0")[:, :], x_tm[0:128, :], "p0"),
               ctx.dma("scalar", S("xs1")[:, :], x_tm[128:256, :], "p1")]
        t_sq = [0, 0]
        t_rd = [0, 0]
        for i in range(NT128):
            par = "p%d" % (i % 2)
            xs = S("xs%d" % (i % 2))
            sq = S("sqj%d" % (i % 2))
            ctx.wait("scalar", par, ldx[i])
            ctx.wait("scalar", "dv", t_rd[i % 2])
            t_sq[i % 2] = ctx.op("scalar", "activation", sq[:, :], xs[:, :],
                                 AF.Square, inc="ac")
            ctx.wait("vector", par, ldx[i])
            ctx.op("vector", "tensor_reduce", S("scol")[:, i:i + 1],
                   xs[:, :], AX.X, ALU.add)
            ctx.wait("vector", "ac", t_sq[i % 2])
            t_rd[i % 2] = ctx.op("vector", "tensor_reduce",
                                 S("sscol")[:, i:i + 1], sq[:, :],
                                 AX.X, ALU.add, inc="dv")
            if i + 2 < NT128:
                ctx.wait("scalar", "dv", t_rd[i % 2])
                ldx.append(ctx.dma("scalar", xs[:, :],
                                   x_tm[128 * (i + 2):128 * (i + 3), :], par))
        ctx.wait("vector", "dv", t_rd[(NT128 - 1) % 2])
        ms = ctx.op("vector", "tensor_scalar_mul", S("mrst")[:, 0:16],
                    S("scol")[:, :], 1.0 / D, inc="dv")
        ctx.wait("vector", "dv", ms)
        ms = ctx.op("vector", "tensor_scalar_mul", S("vtmp")[:, :],
                    S("sscol")[:, :], 1.0 / D, inc="dv")
        ctx.wait("vector", "dv", ms)
        ms = ctx.op("vector", "tensor_tensor", S("mtmp")[:, :],
                    S("mrst")[:, 0:16], S("mrst")[:, 0:16], ALU.mult, inc="dv")
        ctx.wait("vector", "dv", ms)
        dv_var = ctx.op("vector", "tensor_tensor", S("vtmp")[:, :],
                        S("vtmp")[:, :], S("mtmp")[:, :], ALU.subtract,
                        inc="dv")
        ctx.wait("scalar", "dv", dv_var)
        ctx.wait("scalar", "gp", gp_pre)
        ms = ctx.op("scalar", "activation", S("mtmp")[:, :], S("vtmp")[:, :],
                    AF.Ln, bias=S("epsb")[:, 0:1], inc="ac")
        ctx.wait("scalar", "ac", ms)
        ac_std = ctx.op("scalar", "activation", S("mrst")[:, 16:32],
                        S("mtmp")[:, :], AF.Exp, scale=-0.5, inc="ac")
        ctx.wait("vector", "ac", ac_std)
        ms = ctx.op("vector", "tensor_copy", S("tmp64")[0:64, :],
                    S("mrst")[64:128, :], inc="dv")
        ctx.wait("vector", "dv", ms)
        ctx.op("vector", "transpose", S("mrsb")[0:32, 0:32],
               S("mrst")[0:32, 0:32])
        ctx.op("vector", "transpose", S("mrsb")[0:32, 32:64],
               S("mrst")[32:64, 0:32])
        ctx.op("vector", "transpose", S("mrsb")[0:32, 64:96],
               S("tmp64")[0:32, 0:32])
        dv_mr = ctx.op("vector", "transpose", S("mrsb")[0:32, 96:128],
                       S("tmp64")[32:64, 0:32], inc="dv")
        ctx.wait("scalar", "dv", dv_mr)
        ctx.dma("scalar", mr_sp, S("mrsb")[0:32, :], "p2")
        ctx.barrier()

        ld_wq = ctx.dma("gpsimd", S("wq")[:, :, :],
                        wqT.rearrange("c p f -> p c f"), "wg")
        ld_f1 = ctx.dma("gpsimd", S("f1")[:, :], fq1T, "wg")
        ld_f2 = ctx.dma("gpsimd", S("f2")[:, :], fq2T, "wg")
        ctx.dma("sync", S("m_row")[0:1, :],
                mr_sp[0:16, :].rearrange("a b -> (a b)")[None, :], "p5")
        ld_r = ctx.dma("sync", S("r_row")[0:1, :],
                       mr_sp[16:32, :].rearrange("a b -> (a b)")[None, :], "p5")
        ldxf = ctx.dma("sync", S("xf")[:, :, :],
                       x_fm[0].rearrange("c p t -> p c t"), "p0")
        xn_st = [0, 0]
        qm_st = [0, 0]
        pe_tile = [0, 0]
        ac_evac = [0, 0]
        ac_q1 = [0, 0]
        dv_qm = [0, 0]
        for t in range(NT512):
            xn = S("xn%d" % (t % 2))
            qm = S("qm%d" % (t % 2))
            ctx.wait("tensor", "p5", ld_r)
            ctx.wait("tensor", "p0", ldxf)
            pe_mb = ctx.op("tensor", "matmul", A["Mb"][:, :],
                           S("ones1f")[0:1, :],
                           S("m_row")[0:1, 512 * t:512 * (t + 1)],
                           start=True, stop=True)
            pe_rb = ctx.op("tensor", "matmul", A["Rb"][:, :],
                           S("ones1f")[0:1, :],
                           S("r_row")[0:1, 512 * t:512 * (t + 1)],
                           start=True, stop=True, inc="pe")
            ctx.wait("vector", "pe", max(pe_rb, pe_tile[t % 2]))
            ctx.wait("vector", "p%d" % (1 + t % 2), xn_st[t % 2])
            for c in range(16):
                ms = ctx.op("vector", "tensor_tensor", S("t1")[:, :],
                            S("xf")[:, c, :], A["Mb"][:, :], ALU.subtract,
                            inc="dv")
                ctx.wait("vector", "dv", ms)
                dv_xn = ctx.op("vector", "tensor_tensor", xn[:, c, :],
                               S("t1")[:, :], A["Rb"][:, :], ALU.mult,
                               inc="dv" if c == 15 else None)
            ctx.wait("scalar", "dv", dv_xn)
            xn_st[t % 2] = ctx.dma("scalar",
                                   xn_sp[t].rearrange("c p t -> p c t"),
                                   xn[:, :, :], "p%d" % (1 + t % 2))
            if t + 1 < NT512:
                ctx.wait("sync", "dv", dv_xn)
                ldxf = ctx.dma("sync", S("xf")[:, :, :],
                               x_fm[t + 1].rearrange("c p t -> p c t"), "p0")
            ctx.wait("tensor", "wg", max(ld_wq, ld_f1, ld_f2))
            ctx.wait("tensor", "dv", dv_xn)
            ctx.wait("vector", "p7", ld_b2)
            ctx.wait("vector", "p%d" % (3 + t % 2), qm_st[t % 2])
            ac_ev = {}
            pe_q2d = {}
            for h in range(H + 1):
                if h < H:
                    p = h % 2
                    qp = A["qp%d" % p]
                    ctx.wait("tensor", "ac", ac_evac[p])
                    for kc in range(16):
                        ctx.op("tensor", "matmul", qp[:, :],
                               S("wq")[:, kc, 128 * h:128 * (h + 1)],
                               xn[:, kc, :],
                               start=(kc == 0), stop=(kc == 15))
                    pe_q = ctx.mark("tensor", "pe")
                    ctx.wait("scalar", "pe", pe_q)
                    ctx.wait("scalar", "dv", dv_qm[p])
                    ac_evac[p] = ctx.op("scalar", "copy", S("qb%d" % p)[:, :],
                                        qp[:, :], inc="ac")
                    ac_ev[h] = ac_evac[p]
                if h >= 1:
                    hp = h - 1
                    p = hp % 2
                    q1p, q2p = A["q1p%d" % p], A["q2p%d" % p]
                    qb, q1s = S("qb%d" % p), S("q1s%d" % p)
                    ctx.wait("tensor", "ac", max(ac_ev[hp], ac_q1[p]))
                    ctx.wait("tensor", "dv", dv_qm[p])
                    ctx.wait("tensor", "wg", ld_b1)
                    ctx.wait("tensor", "gp", gp_pre)
                    ctx.op("tensor", "matmul", q1p[:, :], S("f1")[:, :],
                           qb[:, :], start=True, stop=False)
                    ctx.op("tensor", "matmul", q1p[:, :], S("b1row")[0:1, :],
                           S("ones5b")[0:1, :], start=False, stop=True)
                    pe_q2 = ctx.op("tensor", "matmul", q2p[:, :], S("f2")[:, :],
                                   qb[:, :], start=True, stop=True, inc="pe")
                    pe_q2d[hp] = pe_q2
                    ctx.wait("scalar", "pe", pe_q2)
                    ac_q1[p] = ctx.op("scalar", "activation", q1s[:, :],
                                      q1p[:, :], AF.Copy, scale=SCALE,
                                      inc="ac")
                    ctx.wait("vector", "ac", ac_q1[p])
                    dv_qm[p] = ctx.op("vector", "scalar_tensor_tensor",
                                      qm[:, hp, :], q2p[:, :], S("bq2")[:, 0:1],
                                      q1s[:, :], ALU.add, ALU.mult, inc="dv")
            pe_tile[t % 2] = pe_q2
            ctx.wait("scalar", "dv", max(dv_qm[0], dv_qm[1]))
            qm_st[t % 2] = ctx.dma("scalar",
                                   qm_sp[t].rearrange("c p t -> p c t"),
                                   qm[:, :, :], "p%d" % (3 + t % 2))
        ctx.barrier()

        ld_wk = ctx.dma("gpsimd", S("wk")[:, :, :],
                        wkT.rearrange("c p f -> p c f"), "wg")
        ld_g1 = ctx.dma("gpsimd", S("g1")[:, :], fk1T, "wg")
        ld_g2 = ctx.dma("gpsimd", S("g2")[:, :], fk2T, "wg")
        ldxn = [ctx.dma("sync", S("kxn0")[:, :, :],
                        xn_sp[0].rearrange("c p t -> p c t"), "p0")]
        km_st = [0, 0]
        kmt_st = [0, 0]
        pe_kt_tile = [0, 0]
        pe_ktile = [0, 0]
        ac_evac = [0, 0]
        ac_k1 = [0, 0]
        dv_km = [0, 0]
        ac_kt = [0, 0]
        for t in range(NT512):
            xn = S("kxn%d" % (t % 2))
            km = S("km%d" % (t % 2))
            if t + 1 < NT512:
                ctx.wait("sync", "pe", pe_ktile[(t + 1) % 2])
                ldxn.append(ctx.dma("sync",
                                    S("kxn%d" % ((t + 1) % 2))[:, :, :],
                                    xn_sp[t + 1].rearrange("c p t -> p c t"),
                                    "p%d" % ((t + 1) % 2)))
            ctx.wait("tensor", "wg", max(ld_wk, ld_g1, ld_g2))
            ctx.wait("tensor", "p%d" % (t % 2), ldxn[t])
            ctx.wait("vector", "p7", ld_b4)
            ctx.wait("vector", "p%d" % (2 + t % 2), km_st[t % 2])
            ctx.wait("vector", "pe", pe_kt_tile[t % 2])
            ac_ev = {}
            for h in range(H + 1):
                if h < H:
                    p = h % 2
                    kp = A["kp%d" % p]
                    ctx.wait("tensor", "ac", ac_evac[p])
                    for kc in range(16):
                        ctx.op("tensor", "matmul", kp[:, :],
                               S("wk")[:, kc, 128 * h:128 * (h + 1)],
                               xn[:, kc, :],
                               start=(kc == 0), stop=(kc == 15))
                    pe_k = ctx.mark("tensor", "pe")
                    ctx.wait("scalar", "pe", pe_k)
                    ctx.wait("scalar", "dv", dv_km[p])
                    ac_evac[p] = ctx.op("scalar", "copy", S("kb%d" % p)[:, :],
                                        kp[:, :], inc="ac")
                    ac_ev[h] = ac_evac[p]
                if h >= 1:
                    hp = h - 1
                    p = hp % 2
                    k1p, k2p = A["k1p%d" % p], A["k2p%d" % p]
                    kb, k1s = S("kb%d" % p), S("k1s%d" % p)
                    ctx.wait("tensor", "ac", max(ac_ev[hp], ac_k1[p]))
                    ctx.wait("tensor", "dv", dv_km[p])
                    ctx.wait("tensor", "wg", ld_b3)
                    ctx.wait("tensor", "gp", gp_pre)
                    ctx.op("tensor", "matmul", k1p[:, :], S("g1")[:, :],
                           kb[:, :], start=True, stop=False)
                    ctx.op("tensor", "matmul", k1p[:, :], S("b3row")[0:1, :],
                           S("ones5b")[0:1, :], start=False, stop=True)
                    pe_k2 = ctx.op("tensor", "matmul", k2p[:, :], S("g2")[:, :],
                                   kb[:, :], start=True, stop=True, inc="pe")
                    ctx.wait("scalar", "pe", pe_k2)
                    ac_k1[p] = ctx.op("scalar", "copy", k1s[:, :], k1p[:, :],
                                      inc="ac")
                    ctx.wait("vector", "ac", ac_k1[p])
                    dv_km[p] = ctx.op("vector", "scalar_tensor_tensor",
                                      km[:, hp, :], k2p[:, :], S("bk2")[:, 0:1],
                                      k1s[:, :], ALU.add, ALU.mult, inc="dv")
            pe_ktile[t % 2] = pe_k2
            ctx.wait("scalar", "dv", max(dv_km[0], dv_km[1]))
            km_st[t % 2] = ctx.dma("scalar",
                                   km_sp[t].rearrange("c p t -> p c t"),
                                   km[:, :, :], "p%d" % (2 + t % 2))
            ctx.wait("tensor", "dv", max(dv_km[0], dv_km[1]))
            ctx.wait("tensor", "wg", ld_identb)
            for cl in range(8):
                kmt = S("kmt%d" % (cl % 2))
                ctx.wait("scalar", "p%d" % (4 + cl % 2), kmt_st[cl % 2])
                for half in range(2):
                    ktp = A["ktp%d" % half]
                    ctx.wait("tensor", "ac", ac_kt[half])
                    for hh in range(8):
                        h = 8 * half + hh
                        ctx.op("tensor", "transpose",
                               ktp[0:64, 128 * hh:128 * (hh + 1)],
                               km[:, h, 64 * cl:64 * (cl + 1)],
                               S("identb")[:, :])
                    pe_t = ctx.mark("tensor", "pe")
                    ctx.wait("scalar", "pe", pe_t)
                    ac_kt[half] = ctx.op(
                        "scalar", "copy",
                        kmt[0:64, 1024 * half:1024 * (half + 1)],
                        ktp[0:64, :], inc="ac")
                ctx.wait("scalar", "ac", ac_kt[1])
                kmt_st[cl % 2] = ctx.dma("scalar", kmt_sp[8 * t + cl],
                                         kmt[0:64, :], "p%d" % (4 + cl % 2))
            pe_kt_tile[t % 2] = pe_t
        ctx.barrier()

        ld_wv = ctx.dma("gpsimd", S("wv")[:, :, :],
                        wvT.rearrange("c p f -> p c f"), "wg")
        ldxn = [ctx.dma("sync", S("vxn0")[:, :, :],
                        xn_sp[0].rearrange("c p t -> p c t"), "p0")]
        v_st = [0, 0]
        pe_vtile = [0, 0]
        ac_v = [0, 0]
        for t in range(NT512):
            xn = S("vxn%d" % (t % 2))
            if t + 1 < NT512:
                ctx.wait("sync", "pe", pe_vtile[(t + 1) % 2])
                ldxn.append(ctx.dma("sync",
                                    S("vxn%d" % ((t + 1) % 2))[:, :, :],
                                    xn_sp[t + 1].rearrange("c p t -> p c t"),
                                    "p%d" % ((t + 1) % 2)))
            ctx.wait("tensor", "p%d" % (t % 2), ldxn[t])
            ctx.wait("tensor", "wg", ld_wv)
            for s in range(4):
                sp = s % 2
                vp = A["vp%d" % sp]
                vb = S("vb%d" % sp)
                ctx.wait("tensor", "ac", ac_v[sp])
                for kc in range(16):
                    for fb in range(4):
                        ctx.op("tensor", "matmul",
                               vp[:, 512 * fb:512 * (fb + 1)],
                               xn[:, kc, 128 * s:128 * (s + 1)],
                               S("wv")[:, kc, 512 * fb:512 * (fb + 1)],
                               start=(kc == 0), stop=(kc == 15))
                pe_v = ctx.mark("tensor", "pe")
                ctx.wait("scalar", "pe", pe_v)
                ctx.wait("scalar", "p%d" % (2 + sp), v_st[sp])
                ac_v[sp] = ctx.op("scalar", "copy", vb[:, :], vp[:, :],
                                  inc="ac")
                ctx.wait("scalar", "ac", ac_v[sp])
                ch = 8 * t + 2 * s
                ctx.dma("scalar", v_sp[ch], vb[0:64, :], "p%d" % (2 + sp))
                v_st[sp] = ctx.dma("scalar", v_sp[ch + 1], vb[64:128, :],
                                   "p%d" % (2 + sp))
            pe_vtile[t % 2] = pe_v
        ctx.barrier()

        ld_kt = [ctx.dma("sync", S("tkmt0")[0:64, :], kmt_sp[0], "p0")]
        ld_tv = [ctx.dma("sync", S("tv0")[0:64, :], v_sp[0], "p0")]
        pe_tc = [0, 0]
        for c in range(NCH):
            kmt = S("tkmt%d" % (c % 2))
            tv = S("tv%d" % (c % 2))
            if c + 1 < NCH:
                par = "p%d" % ((c + 1) % 2)
                ctx.wait("sync", "pe", pe_tc[(c + 1) % 2])
                ld_kt.append(ctx.dma("sync",
                                     S("tkmt%d" % ((c + 1) % 2))[0:64, :],
                                     kmt_sp[c + 1], par))
                ld_tv.append(ctx.dma("sync",
                                     S("tv%d" % ((c + 1) % 2))[0:64, :],
                                     v_sp[c + 1], par))
            ctx.wait("tensor", "p%d" % (c % 2), max(ld_kt[c], ld_tv[c]))
            for h in range(H):
                ctx.op("tensor", "matmul", A["Tp"][:, 128 * h:128 * (h + 1)],
                       kmt[0:64, 128 * h:128 * (h + 1)],
                       tv[0:64, 128 * h:128 * (h + 1)],
                       start=(c == 0 and h % 4 == 0),
                       stop=(c == NCH - 1 and h % 4 == 3))
            pe_tc[c % 2] = ctx.mark("tensor", "pe")
        pe_T = pe_tc[(NCH - 1) % 2]
        ctx.wait("scalar", "pe", pe_T)
        ac_T = ctx.op("scalar", "copy", S("ttsb")[:, :], A["Tp"][:, :],
                      inc="ac")
        ctx.wait("vector", "ac", ac_T)
        ctx.wait("vector", "p7", ld_flags)
        dv_T = ctx.op("vector", "tensor_scalar_mul", S("ttsb")[:, :],
                      S("ttsb")[:, :], S("flags")[:, 0:1], inc="dv")
        ctx.wait("scalar", "dv", dv_T)
        g_in = ctx.dma("scalar", cc_in[:, :], S("ttsb")[:, :], "wg")
        ctx.wait("gpsimd", "wg", g_in)
        cc_t = ctx.collective([cc_in.ap().opt()], [cc_out.ap().opt()], "cc")
        ctx.wait("scalar", "cc", cc_t)
        g_out = ctx.dma("scalar", S("s0f")[:, :], cc_out[:, :], "wg")
        ctx.wait("vector", "wg", g_out)
        dv_s0 = ctx.op("vector", "tensor_scalar_mul", S("s0b")[:, :],
                       S("s0f")[:, :], S("flags")[:, 1:2], inc="dv")
        for r in range(RANK):
            kw = {} if r == 0 else {"accum_op": ALU.add}
            fa_t = ctx.dma("gpsimd", S("facc2")[:, :, :],
                           factsl[r].rearrange("(c p) e -> p c e", p=128),
                           "fa", **kw)
            ctx.wait("gpsimd", "fa", fa_t)
        g_fc = ctx.dma("gpsimd", cc_f_in[:, :],
                       S("facc2")[:, :, :].rearrange("p c e -> p (c e)"), "p6")
        ctx.wait("gpsimd", "p6", g_fc)
        ccf = ctx.inc("cc2", 1)
        if ctx.eng == "gpsimd":
            ctx.e.collective_compute(
                "AllGather", ALU.bypass,
                replica_groups=[[0, 1, 2, 3, 4, 5, 6, 7]],
                ins=[cc_f_in.ap().opt()], outs=[cc_f_out.ap().opt()],
            ).then_inc(ctx.sems["cc2"])
        ld_wo = ctx.dma("gpsimd", S("wo")[:, :, :],
                        woT.rearrange("c p f -> p c f"), "wg")
        ctx.barrier(exclude=("fa", "wg", "p6"))

        ld_aq = [ctx.dma("sync", S("aqm0")[:, :, :],
                         qm_sp[0].rearrange("c p t -> p c t"), "p2")]
        ld_ak = [ctx.dma("sync", S("akm0")[:, :, :],
                         km_sp[0].rearrange("c p t -> p c t"), "p2")]
        ld_av = [ctx.dma("sync", S("av0")[:, :],
                         v_sp[0:2].rearrange("c p f -> (c p) f"), "p0")]
        ld_akt = [ctx.dma("sync", S("akt0")[:, :],
                          kmt_sp[0:2].rearrange("c p f -> (c p) f"), "p0")]
        ctx.wait("tensor", "dv", dv_s0)
        ctx.wait("tensor", "wg", ld_identb)
        for h in range(H):
            ctx.op("tensor", "matmul", A["St"][:, 128 * h:128 * (h + 1)],
                   S("identb")[:, :], S("s0b")[:, 128 * h:128 * (h + 1)],
                   start=(h % 4 == 0), stop=False)
        pe_init = ctx.mark("tensor", "pe")
        kv_mark = [pe_init] * 4
        dv_mask = {}
        dv_mask_last = 0
        ac_snap_sg = [0, 0, 0, 0]
        ac_oev = [0, 0]
        dv_on_gp = [0, 0]
        dv_on_sg = {}
        pe_of_pair = [0, 0]
        ac_ofm = 0
        st_of = [0, 0]
        NPAIR = NCH // 2

        def scores(jn):
            nonlocal dv_mask_last
            t5n, jln = jn // 4, jn % 4
            qmn = S("aqm%d" % (t5n % 2))
            kmn = S("akm%d" % (t5n % 2))
            tsn = slice(128 * jln, 128 * (jln + 1))
            sbf = S("sbf%d" % (jn % 2))
            for sg in range(4):
                ctx.wait("tensor", "dv", dv_mask_last)
                for hh in range(4):
                    h = 4 * sg + hh
                    ctx.op("tensor", "matmul",
                           A["spp"][:, 128 * hh:128 * (hh + 1)],
                           kmn[:, h, tsn], qmn[:, h, tsn],
                           start=(hh == 0), stop=(hh == 3))
                pe_sp = ctx.mark("tensor", "pe")
                ctx.wait("vector", "pe", pe_sp)
                ctx.wait("vector", "p7", ld_mask)
                dv_mask[(jn, sg)] = ctx.op(
                    "vector", "tensor_tensor",
                    sbf[:, 512 * sg:512 * (sg + 1)],
                    A["spp"][:, :], S("maskb")[:, :], ALU.mult, inc="dv")
                dv_mask_last = dv_mask[(jn, sg)]

        def transposes(jp):
            nonlocal ac_ofm, pe_of_pair, st_of
            par = jp % 2
            onb = S("onb%d" % par)
            ofm = S("ofm%d" % par)
            for sg in range(4):
                ctx.wait("tensor", "dv", dv_on_sg[(jp, sg)])
                ctx.wait("tensor", "ac", ac_ofm)
                for hh in range(4):
                    ctx.op("tensor", "transpose",
                           A["ofp"][:, 128 * hh:128 * (hh + 1)],
                           onb[:, 512 * sg + 128 * hh:512 * sg + 128 * hh + 128],
                           S("identb")[:, :])
                pe_of = ctx.mark("tensor", "pe")
                ctx.wait("scalar", "pe", pe_of)
                ctx.wait("scalar", "p%d" % (4 + par), st_of[par])
                ac_ofm = ctx.op("scalar", "copy",
                                ofm[:, 4 * sg:4 * (sg + 1), :],
                                A["ofp"][:, :]
                                .rearrange("p (h t) -> p h t", h=4),
                                inc="ac")
            pe_of_pair[par] = pe_of
            ctx.wait("scalar", "ac", ac_ofm)
            st_of[par] = ctx.dma("scalar", o_sp[jp], ofm[:, :, :],
                                 "p%d" % (4 + par))

        ctx.wait("tensor", "p2", ld_ak[0])
        scores(0)
        for j in range(NPAIR):
            t5, jl = j // 4, j % 4
            qm = S("aqm%d" % (t5 % 2))
            akt = S("akt%d" % (j % 2))
            av = S("av%d" % (j % 2))
            if jl == 0 and t5 + 1 < NT512:
                par = "p%d" % (2 + (t5 + 1) % 2)
                ctx.wait("sync", "pe", kv_mark[3])
                ld_aq.append(ctx.dma("sync",
                                     S("aqm%d" % ((t5 + 1) % 2))[:, :, :],
                                     qm_sp[t5 + 1].rearrange("c p t -> p c t"),
                                     par))
                ld_ak.append(ctx.dma("sync",
                                     S("akm%d" % ((t5 + 1) % 2))[:, :, :],
                                     km_sp[t5 + 1].rearrange("c p t -> p c t"),
                                     par))
            if j + 1 < NPAIR:
                par = "p%d" % ((j + 1) % 2)
                ctx.wait("sync", "pe", kv_mark[3])
                ld_akt.append(ctx.dma(
                    "sync", S("akt%d" % ((j + 1) % 2))[:, :],
                    kmt_sp[2 * j + 2:2 * j + 4].rearrange("c p f -> (c p) f"),
                    par))
                ld_av.append(ctx.dma(
                    "sync", S("av%d" % ((j + 1) % 2))[:, :],
                    v_sp[2 * j + 2:2 * j + 4].rearrange("c p f -> (c p) f"),
                    par))
            tsl = slice(128 * jl, 128 * (jl + 1))
            ctx.wait("tensor", "p%d" % (2 + t5 % 2), ld_ak[t5])
            ctx.wait("tensor", "p%d" % (j % 2), max(ld_av[j], ld_akt[j]))
            sbf = S("sbf%d" % (j % 2))
            for sg in range(4):
                ctx.wait("scalar", "pe", kv_mark[sg])
                ac_snap_sg[sg] = ctx.op(
                    "scalar", "copy", S("snap")[:, 512 * sg:512 * (sg + 1)],
                    A["St"][:, 512 * sg:512 * (sg + 1)], inc="ac")
            for sg in range(4):
                gp_ = (4 * j + sg) % 2
                op = A["op%d" % gp_]
                obf = S("obf%d" % gp_)
                ctx.wait("tensor", "ac", max(ac_snap_sg[sg], ac_oev[gp_]))
                for hh in range(4):
                    h = 4 * sg + hh
                    ctx.op("tensor", "matmul",
                           op[:, 128 * hh:128 * (hh + 1)],
                           qm[:, h, tsl],
                           S("snap")[:, 128 * h:128 * (h + 1)],
                           start=(hh == 0), stop=False)
                ctx.wait("tensor", "dv", dv_mask[(j, sg)])
                for hh in range(4):
                    h = 4 * sg + hh
                    ctx.op("tensor", "matmul",
                           op[:, 128 * hh:128 * (hh + 1)],
                           sbf[:, 512 * sg + 128 * hh:512 * sg + 128 * hh + 128],
                           av[:, 128 * h:128 * (h + 1)],
                           start=False, stop=(hh == 3))
                pe_o = ctx.mark("tensor", "pe")
                for hh in range(4):
                    h = 4 * sg + hh
                    ctx.op("tensor", "matmul",
                           A["St"][:, 128 * h:128 * (h + 1)],
                           akt[:, 128 * h:128 * (h + 1)],
                           av[:, 128 * h:128 * (h + 1)],
                           start=False,
                           stop=(j == NPAIR - 1 and hh == 3))
                kv_mark[sg] = ctx.mark("tensor", "pe")
                ctx.wait("scalar", "pe", pe_o)
                ctx.wait("scalar", "dv", dv_on_gp[gp_])
                ac_oev[gp_] = ctx.op("scalar", "copy", obf[:, :], op[:, :],
                                     inc="ac")
                ctx.wait("vector", "ac", ac_oev[gp_])
                ms = ctx.op("vector", "tensor_tensor", S("osq")[:, :],
                            obf[:, :], obf[:, :], ALU.mult, inc="dv")
                ctx.wait("vector", "dv", ms)
                ctx.op("vector", "tensor_reduce",
                       S("ss8")[:, :].unsqueeze(2),
                       S("osq")[:, :].rearrange("p (g e) -> p g e", g=4),
                       AX.X, ALU.add)
                dv_ss = ctx.mark("vector", "dv")
                ctx.wait("scalar", "dv", dv_ss)
                ms = ctx.op("scalar", "activation", S("sd8")[:, :],
                            S("ss8")[:, :], AF.Ln, scale=1.0 / DK,
                            bias=S("epsb")[:, 0:1], inc="ac")
                ctx.wait("scalar", "ac", ms)
                ac_sd = ctx.op("scalar", "activation", S("rr8")[:, :],
                               S("sd8")[:, :], AF.Exp, scale=-0.5, inc="ac")
                ctx.wait("vector", "ac", ac_sd)
                ctx.wait("vector", "pe", pe_of_pair[j % 2])
                dv_on = ctx.op("vector", "tensor_tensor",
                               S("onb%d" % (j % 2))
                               [:, 512 * sg:512 * (sg + 1)]
                               .rearrange("p (g e) -> p g e", g=4),
                               obf[:, :].rearrange("p (g e) -> p g e", g=4),
                               S("rr8")[:, :].unsqueeze(2)
                               .broadcast_to([128, 4, DK]),
                               ALU.mult, inc="dv")
                dv_on_gp[gp_] = dv_on
                dv_on_sg[(j, sg)] = dv_on
            if j + 1 < NPAIR:
                ctx.wait("tensor", "p%d" % (2 + ((j + 1) // 4) % 2),
                         ld_ak[(j + 1) // 4])
                scores(j + 1)
            if j >= 1:
                transposes(j - 1)
        transposes(NPAIR - 1)
        ctx.wait("vector", "wg", ld_wo)
        ctx.wait("vector", "p7", ld_nw)
        dv_wo = ctx.op("vector", "tensor_scalar_mul",
                       S("wo")[:, :, :].rearrange("p c f -> p (c f)"),
                       S("wo")[:, :, :].rearrange("p c f -> p (c f)"),
                       S("nw")[:, 0:1], inc="dv")
        ctx.wait("scalar", "pe", max(kv_mark))
        ms = ctx.op("scalar", "copy", S("snap")[:, :], A["St"][:, :], inc="ac")
        ctx.wait("scalar", "ac", ms)
        ctx.dma("scalar", dbg5, S("snap")[:, :], "p6")
        ctx.barrier(exclude=("fa",))

        ccv = cc_f_out.ap().rearrange("(r p) (c e) -> r p c e", r=8, c=16)
        ctx.wait("sync", "cc2", ccf)
        for r in range(8):
            ld_F = ctx.dma("sync", S("Fb")[:, :, 256 * r:256 * (r + 1)],
                           ccv[r], "p7")
        ld_o = [ctx.dma("sync", S("osb0")[:, :, :], o_sp[0], "p0")]
        ld_xt = [ctx.dma("scalar", S("xt0")[:, :], x_tm[0:128, :], "p2")]
        ld_ox = [ctx.dma("sync", S("oxn")[:, :, :],
                         xn_sp[0].rearrange("c p t -> p c t"), "p6")]
        dv_out = [0, 0]
        st_out = [0, 0]
        dv_psum = [0, 0]
        pe_otile = [0, 0]
        for t in range(NT128):
            t5, s = t // 4, t % 4
            osb = S("osb%d" % (t % 2))
            xt = S("xt%d" % (t % 2))
            if t + 1 < NT128:
                ctx.wait("sync", "pe", pe_otile[(t + 1) % 2])
                ld_o.append(ctx.dma("sync",
                                    S("osb%d" % ((t + 1) % 2))[:, :, :],
                                    o_sp[t + 1], "p%d" % ((t + 1) % 2)))
                ctx.wait("scalar", "dv", dv_out[(t + 1) % 2])
                ld_xt.append(ctx.dma("scalar",
                                     S("xt%d" % ((t + 1) % 2))[:, :],
                                     x_tm[128 * (t + 1):128 * (t + 2), :],
                                     "p%d" % (2 + (t + 1) % 2)))
            ctx.wait("tensor", "p%d" % (t % 2), ld_o[t])
            ctx.wait("tensor", "p6", ld_ox[t5])
            ctx.wait("tensor", "p7", ld_F)
            ctx.wait("tensor", "dv", dv_wo)
            ctx.wait("tensor", "wg", ld_wo)
            ctx.wait("vector", "p%d" % (2 + t % 2), ld_xt[t])
            ctx.wait("vector", "p%d" % (4 + t % 2), st_out[t % 2])
            for fbp in range(2):
                pp = (2 * t + fbp) % 2
                ap, fp = A["ap%d" % pp], A["fp%d" % pp]
                ctx.wait("tensor", "dv", dv_psum[pp])
                for fb2 in range(2):
                    fb = 2 * fbp + fb2
                    fsl = slice(512 * fb, 512 * (fb + 1))
                    psl = slice(512 * fb2, 512 * (fb2 + 1))
                    for h in range(H):
                        ctx.op("tensor", "matmul", ap[:, psl],
                               osb[:, h, :], S("wo")[:, h, fsl],
                               start=(h == 0), stop=False)
                    ctx.op("tensor", "matmul", ap[:, psl],
                           S("ones1b")[0:1, :], S("brow")[0:1, fsl],
                           start=False, stop=True)
                    for kc in range(16):
                        ctx.op("tensor", "matmul", fp[:, psl],
                               S("oxn")[:, kc, 128 * s:128 * (s + 1)],
                               S("Fb")[:, kc, fsl],
                               start=(kc == 0), stop=(kc == 15))
                pe_fb = ctx.mark("tensor", "pe")
                ctx.wait("vector", "pe", pe_fb)
                for fb2 in range(2):
                    fb = 2 * fbp + fb2
                    fsl = slice(512 * fb, 512 * (fb + 1))
                    psl = slice(512 * fb2, 512 * (fb2 + 1))
                    ms = ctx.op("vector", "scalar_tensor_tensor", S("tf")[:, :],
                                fp[:, psl], FSC, xt[:, fsl],
                                ALU.mult, ALU.add, inc="dv")
                    ctx.wait("vector", "dv", ms)
                    dv_last = ctx.op("vector", "tensor_tensor",
                                     S("ot%d" % (t % 2))[:, fsl],
                                     S("tf")[:, :], ap[:, psl], ALU.add,
                                     inc="dv")
                dv_psum[pp] = dv_last
            pe_otile[t % 2] = pe_fb
            dv_out[t % 2] = dv_last
            if s == 3 and t5 + 1 < NT512:
                ctx.wait("sync", "pe", pe_fb)
                ld_ox.append(ctx.dma("sync", S("oxn")[:, :, :],
                                     xn_sp[t5 + 1].rearrange("c p t -> p c t"),
                                     "p6"))
            ctx.wait("scalar", "dv", dv_out[t % 2])
            st_out[t % 2] = ctx.dma("scalar", out[128 * t:128 * (t + 1), :],
                                    S("ot%d" % (t % 2))[:, :],
                                    "p%d" % (4 + t % 2))
        ctx.barrier()

    with nc.Block() as block:
        for eng in ENGS:
            def body(e, eng=eng):
                emit_program(Ctx(nc, sems, eng, e))
            getattr(block, eng)(body)
    sem_stack.close()
    persist.close()
    return nc


def _layout_inputs(inputs):
    x = np.ascontiguousarray(np.asarray(inputs["L0_attn_x"], dtype=np.float32))
    xr = x.reshape(16384, D)
    shared = {
        "wqT": np.ascontiguousarray(np.asarray(inputs["L0_attn_wq"]).T).reshape(16, 128, D),
        "wkT": np.ascontiguousarray(np.asarray(inputs["L0_attn_wk"]).T).reshape(16, 128, D),
        "wvT": np.ascontiguousarray(np.asarray(inputs["L0_attn_wv"]).T).reshape(16, 128, D),
        "woT": np.ascontiguousarray(np.asarray(inputs["L0_attn_wo"]).T).reshape(16, 128, D),
        "fq1T": np.ascontiguousarray(np.asarray(inputs["L0_attn_fq1_w"]).T),
        "fq2T": np.ascontiguousarray(np.asarray(inputs["L0_attn_fq2_w"]).T),
        "fk1T": np.ascontiguousarray(np.asarray(inputs["L0_attn_fk1_w"]).T),
        "fk2T": np.ascontiguousarray(np.asarray(inputs["L0_attn_fk2_w"]).T),
        "fq1b": np.asarray(inputs["L0_attn_fq1_b"], dtype=np.float32),
        "fq2b": np.asarray(inputs["L0_attn_fq2_b"], dtype=np.float32),
        "fk1b": np.asarray(inputs["L0_attn_fk1_b"], dtype=np.float32),
        "fk2b": np.asarray(inputs["L0_attn_fk2_b"], dtype=np.float32),
        "norw": np.asarray(inputs["L0_attn_norm_w"], dtype=np.float32),

        "fbias": np.asarray(inputs["L0_ffn_bias"], dtype=np.float32),
        "maskT": np.tile(np.block(
            [[np.triu(np.ones((64, 64), np.float32)), np.ones((64, 64), np.float32)],
             [np.zeros((64, 64), np.float32), np.triu(np.ones((64, 64), np.float32))]]),
            (1, 4)),
        "ident": np.eye(128, dtype=np.float32),
    }
    fact = np.asarray(inputs["L0_ffn_factors"], dtype=np.float32)
    in_maps = []
    for c in range(8):
        x_tm = np.ascontiguousarray(xr[2048 * c:2048 * (c + 1)])
        m = dict(shared)
        m["factsl"] = np.ascontiguousarray(fact[:, :, 256 * c:256 * (c + 1)])
        m["x_tm"] = x_tm
        m["x_fm"] = np.ascontiguousarray(
            x_tm.reshape(4, 512, 16, 128).transpose(0, 2, 3, 1))
        m["cflag"] = np.full((128, 1), 1.0 if c % 2 == 0 else 0.0, np.float32)
        m["uflag"] = np.full((128, 1), 0.0 if c % 2 == 0 else 1.0, np.float32)
        in_maps.append(m)
    return in_maps


def kernel(**inputs):
    in_maps = _layout_inputs(inputs)
    nc = build_program()
    res = run_bass_kernel_spmd(nc, in_maps, core_ids=list(range(8)))
    out = np.empty((4, 4096, D), np.float32)
    for c in range(8):
        out[c // 2, 2048 * (c % 2):2048 * (c % 2 + 1), :] = res.results[c]["out"]
    return out


# version 09 (final)
# speedup vs baseline: 1.0759x; 1.0177x over previous
import math
import numpy as np
import concourse.bass as bass
import concourse.mybir as mybir
from concourse.bass_utils import run_bass_kernel_spmd
from contextlib import ExitStack

F32 = mybir.dt.float32
BF16 = mybir.dt.bfloat16
AF = mybir.ActivationFunctionType
ALU = mybir.AluOpType
AX = mybir.AxisListType

D = 2048
H = 16
DK = 128
CH = 64
T = 2048
NT128 = 16
NT512 = 4
NCH = 32
EPS = 1e-5
SCALE = DK ** (-0.5)
RANK = 8
FSC = 1.0 / math.sqrt(RANK)

ENGS = ("sync", "scalar", "vector", "tensor", "gpsimd")
DMA_SEMS = ("p0", "p1", "p2", "p3", "p4", "p5", "p6", "p7", "wg", "fa")


class Ctx:

    def __init__(self, nc, sems, eng, e):
        self.nc = nc
        self.sems = sems
        self.cnt = {k: 0 for k in sems}
        self.eng = eng
        self.e = e
        self.last = None
        self.had_compute = set()
        self.pend = {e_: False for e_ in ENGS}

    def inc(self, sem, v):
        self.cnt[sem] += v
        return self.cnt[sem]

    def op(self, eng, fname, *a, inc=None, **kw):
        tgt = self.inc(inc, 1) if inc else None
        self.had_compute.add(eng)
        self.pend[eng] = inc is None
        if self.eng == eng:
            inst = getattr(self.e, fname)(*a, **kw)
            if inc:
                inst.then_inc(self.sems[inc], 1)
            self.last = inst
        return tgt

    def mark(self, eng, sem):
        tgt = self.inc(sem, 1)
        self.pend[eng] = False
        if self.eng == eng:
            self.last.then_inc(self.sems[sem], 1)
        return tgt

    def dma(self, eng, out, in_, sem, **kw):
        tgt = self.inc(sem, 16)
        if self.eng == eng:
            self.e.dma_start(out=out, in_=in_, **kw).then_inc(self.sems[sem], 16)
        return tgt

    def collective(self, ins, outs, sem):
        tgt = self.inc(sem, 1)
        if self.eng == "gpsimd":
            self.e.collective_compute(
                "AllReduce", ALU.add,
                replica_groups=[[0, 1], [2, 3], [4, 5], [6, 7]],
                ins=ins, outs=outs,
            ).then_inc(self.sems[sem])
        return tgt

    def wait(self, eng, sem, val):
        if val and self.eng == eng:
            self.e.wait_ge(self.sems[sem], val)

    def barrier(self, exclude=()):
        for e in ENGS:
            if e in self.had_compute and self.pend[e]:
                self.inc("bar", 1)
                self.pend[e] = False
                if self.eng == e:
                    self.last.then_inc(self.sems["bar"], 1)
        self.had_compute = set()
        tgt = self.cnt["bar"]
        for e in ENGS:
            self.wait(e, "bar", tgt)
            for s in ("pe", "ac", "dv", "gp"):
                self.wait(e, s, self.cnt[s])
            for s in DMA_SEMS:
                if s not in exclude:
                    self.wait(e, s, self.cnt[s])


def build_program(dump=()):
    nc = bass.Bass("TRN2", target_bir_lowering=False, debug=False)

    def dram(name, shape, dtype=F32):
        kind = "ExternalOutput" if name in dump else "Internal"
        return nc.dram_tensor(name, shape, dtype, kind=kind)

    x_tm = nc.dram_tensor("x_tm", [T, D], F32, kind="ExternalInput").ap()
    x_fm = nc.dram_tensor("x_fm", [NT512, 16, 128, 512], F32, kind="ExternalInput").ap()
    wqT = nc.dram_tensor("wqT", [16, 128, D], F32, kind="ExternalInput").ap()
    wkT = nc.dram_tensor("wkT", [16, 128, D], F32, kind="ExternalInput").ap()
    wvT = nc.dram_tensor("wvT", [16, 128, D], F32, kind="ExternalInput").ap()
    woT = nc.dram_tensor("woT", [16, 128, D], F32, kind="ExternalInput").ap()
    fq1T = nc.dram_tensor("fq1T", [128, 128], F32, kind="ExternalInput").ap()
    fq2T = nc.dram_tensor("fq2T", [128, 128], F32, kind="ExternalInput").ap()
    fk1T = nc.dram_tensor("fk1T", [128, 128], F32, kind="ExternalInput").ap()
    fk2T = nc.dram_tensor("fk2T", [128, 128], F32, kind="ExternalInput").ap()
    fq1b = nc.dram_tensor("fq1b", [128], F32, kind="ExternalInput").ap()
    fq2b = nc.dram_tensor("fq2b", [128], F32, kind="ExternalInput").ap()
    fk1b = nc.dram_tensor("fk1b", [128], F32, kind="ExternalInput").ap()
    fk2b = nc.dram_tensor("fk2b", [128], F32, kind="ExternalInput").ap()
    norw = nc.dram_tensor("norw", [128], F32, kind="ExternalInput").ap()
    factsl = nc.dram_tensor("factsl", [RANK, D, 256], F32, kind="ExternalInput").ap()
    fbias = nc.dram_tensor("fbias", [D], F32, kind="ExternalInput").ap()
    maskT = nc.dram_tensor("maskT", [128, 512], F32, kind="ExternalInput").ap()
    ident = nc.dram_tensor("ident", [128, 128], F32, kind="ExternalInput").ap()
    cflag = nc.dram_tensor("cflag", [128, 1], F32, kind="ExternalInput").ap()
    uflag = nc.dram_tensor("uflag", [128, 1], F32, kind="ExternalInput").ap()

    out = nc.dram_tensor("out", [T, D], F32, kind="ExternalOutput").ap()

    mr_sp = dram("mr_sp", [32, 128]).ap()
    xn_sp = dram("xn_sp", [NT512, 16, 128, 512], BF16).ap()
    qm_sp = dram("qm_sp", [NT512, 16, 128, 512], BF16).ap()
    km_sp = dram("km_sp", [NT512, 16, 128, 512], BF16).ap()
    kmt_sp = dram("kmt_sp", [NCH, 64, D], BF16).ap()
    v_sp = dram("v_sp", [NCH, 64, D], BF16).ap()
    o_sp = dram("o_sp", [NT128, 128, 16, 128], BF16).ap()
    cc_f_in = dram("cc_f_in", [128, 4096], BF16)
    cc_f_out = dram("cc_f_out", [1024, 4096], BF16)
    cc_in = dram("cc_in", [128, H * 128])
    cc_out = dram("cc_out", [128, H * 128])
    dbg5 = dram("dbg5", [128, D], BF16).ap()

    A = {}

    def alloc(stack, name, shape, dtype):
        A[name] = stack.enter_context(nc.sbuf_tensor(name, shape, dtype))

    def palloc(stack, name, shape, dtype=F32):
        A[name] = stack.enter_context(nc.psum_tensor(name, shape, dtype))

    persist = ExitStack()
    alloc(persist, "identb", [128, 128], BF16)
    alloc(persist, "ones1f", [1, 128], F32)
    alloc(persist, "ones1b", [1, 128], BF16)
    alloc(persist, "epsb", [128, 1], F32)
    alloc(persist, "b1row", [1, 128], BF16)
    alloc(persist, "b3row", [1, 128], BF16)
    alloc(persist, "ones5b", [1, 512], BF16)
    alloc(persist, "bq2", [128, 1], F32)
    alloc(persist, "bk2", [128, 1], F32)
    alloc(persist, "nw", [128, 1], F32)
    alloc(persist, "flags", [128, 2], F32)
    alloc(persist, "maskb", [128, 512], F32)
    alloc(persist, "brow", [1, D], BF16)

    sS = ExitStack()
    alloc(sS, "xs0", [128, D], F32)
    alloc(sS, "xs1", [128, D], F32)
    alloc(sS, "sqj0", [128, D], F32)
    alloc(sS, "sqj1", [128, D], F32)
    alloc(sS, "scol", [128, 16], F32)
    alloc(sS, "sscol", [128, 16], F32)
    alloc(sS, "mtmp", [128, 16], F32)
    alloc(sS, "vtmp", [128, 16], F32)
    alloc(sS, "mrst", [128, 32], F32)
    alloc(sS, "mrsb", [32, 128], F32)
    alloc(sS, "tmp64", [64, 32], F32)
    sS.close()

    sQ = ExitStack()
    alloc(sQ, "m_row", [1, T], F32)
    alloc(sQ, "r_row", [1, T], F32)
    alloc(sQ, "wq", [128, 16, D], BF16)
    alloc(sQ, "f1", [128, 128], BF16)
    alloc(sQ, "f2", [128, 128], BF16)
    alloc(sQ, "xf", [128, 16, 512], F32)
    alloc(sQ, "xn0", [128, 16, 512], BF16)
    alloc(sQ, "xn1", [128, 16, 512], BF16)
    alloc(sQ, "t1", [128, 512], F32)
    alloc(sQ, "qb0", [128, 512], BF16)
    alloc(sQ, "qb1", [128, 512], BF16)
    alloc(sQ, "q1s0", [128, 512], F32)
    alloc(sQ, "q1s1", [128, 512], F32)
    alloc(sQ, "qm0", [128, 16, 512], BF16)
    alloc(sQ, "qm1", [128, 16, 512], BF16)
    palloc(sQ, "Mb", [128, 512])
    palloc(sQ, "Rb", [128, 512])
    palloc(sQ, "qp0", [128, 512])
    palloc(sQ, "qp1", [128, 512])
    palloc(sQ, "q1p0", [128, 512])
    palloc(sQ, "q1p1", [128, 512])
    palloc(sQ, "q2p0", [128, 512])
    palloc(sQ, "q2p1", [128, 512])
    sQ.close()

    sK = ExitStack()
    alloc(sK, "wk", [128, 16, D], BF16)
    alloc(sK, "g1", [128, 128], BF16)
    alloc(sK, "g2", [128, 128], BF16)
    alloc(sK, "kxn0", [128, 16, 512], BF16)
    alloc(sK, "kxn1", [128, 16, 512], BF16)
    alloc(sK, "kb0", [128, 512], BF16)
    alloc(sK, "kb1", [128, 512], BF16)
    alloc(sK, "k1s0", [128, 512], F32)
    alloc(sK, "k1s1", [128, 512], F32)
    alloc(sK, "km0", [128, 16, 512], BF16)
    alloc(sK, "km1", [128, 16, 512], BF16)
    alloc(sK, "kmt0", [64, D], BF16)
    alloc(sK, "kmt1", [64, D], BF16)
    palloc(sK, "kp0", [128, 512])
    palloc(sK, "kp1", [128, 512])
    palloc(sK, "k1p0", [128, 512])
    palloc(sK, "k1p1", [128, 512])
    palloc(sK, "k2p0", [128, 512])
    palloc(sK, "k2p1", [128, 512])
    palloc(sK, "ktp0", [64, 1024], BF16)
    palloc(sK, "ktp1", [64, 1024], BF16)
    sK.close()

    sV = ExitStack()
    alloc(sV, "wv", [128, 16, D], BF16)
    alloc(sV, "vxn0", [128, 16, 512], BF16)
    alloc(sV, "vxn1", [128, 16, 512], BF16)
    alloc(sV, "vb0", [128, D], BF16)
    alloc(sV, "vb1", [128, D], BF16)
    palloc(sV, "vp0", [128, D])
    palloc(sV, "vp1", [128, D])
    sV.close()

    sTA = ExitStack()
    alloc(sTA, "s0b", [128, H * 128], BF16)
    alloc(sTA, "facc2", [128, 16, 256], F32)

    sT = ExitStack()
    alloc(sT, "tkmt0", [128, D], BF16)
    alloc(sT, "tkmt1", [128, D], BF16)
    alloc(sT, "tv0", [128, D], BF16)
    alloc(sT, "tv1", [128, D], BF16)
    alloc(sT, "ttsb", [128, H * 128], F32)
    alloc(sT, "s0f", [128, H * 128], F32)
    palloc(sT, "Tp", [128, H * 128])
    sT.close()

    sA = ExitStack()
    alloc(sA, "aqm0", [128, 16, 512], BF16)
    alloc(sA, "aqm1", [128, 16, 512], BF16)
    alloc(sA, "akm0", [128, 16, 512], BF16)
    alloc(sA, "akm1", [128, 16, 512], BF16)
    alloc(sA, "av0", [128, D], BF16)
    alloc(sA, "av1", [128, D], BF16)
    alloc(sA, "akt0", [128, D], BF16)
    alloc(sA, "akt1", [128, D], BF16)
    alloc(sA, "snap", [128, H * 128], BF16)
    alloc(sA, "sbf0", [128, 2048], BF16)
    alloc(sA, "sbf1", [128, 2048], BF16)
    alloc(sA, "obf0", [128, 512], BF16)
    alloc(sA, "obf1", [128, 512], BF16)
    alloc(sA, "osq", [128, 512], F32)
    alloc(sA, "ss8", [128, 4], F32)
    alloc(sA, "sd8", [128, 4], F32)
    alloc(sA, "rr8", [128, 4], F32)
    alloc(sA, "onb0", [128, 2048], BF16)
    alloc(sA, "onb1", [128, 2048], BF16)
    alloc(sA, "ofm0", [128, 16, 128], BF16)
    alloc(sA, "ofm1", [128, 16, 128], BF16)
    palloc(sA, "St", [128, H * 128])
    palloc(sA, "spp", [128, 512])
    palloc(sA, "op0", [128, 512])
    palloc(sA, "op1", [128, 512])
    palloc(sA, "ofp", [128, 512], BF16)
    sA.close()
    sTA.close()

    sR = ExitStack()
    A["wo"] = sR.enter_context(nc.sbuf_tensor("wo", [128, 16, D], BF16,
                                              side="right"))
    sO = ExitStack()
    alloc(sO, "Fb", [128, 16, D], BF16)
    alloc(sO, "osb0", [128, 16, 128], BF16)
    alloc(sO, "osb1", [128, 16, 128], BF16)
    alloc(sO, "oxn", [128, 16, 512], BF16)
    alloc(sO, "xt0", [128, D], F32)
    alloc(sO, "xt1", [128, D], F32)
    alloc(sO, "ot0", [128, D], F32)
    alloc(sO, "ot1", [128, D], F32)
    alloc(sO, "tf", [128, 512], F32)
    palloc(sO, "ap0", [128, 1024])
    palloc(sO, "ap1", [128, 1024])
    palloc(sO, "fp0", [128, 1024])
    palloc(sO, "fp1", [128, 1024])
    sO.close()
    sR.close()

    sem_stack = ExitStack()
    sems = {}
    for s in DMA_SEMS + ("pe", "ac", "dv", "gp", "cc", "cc2", "bar"):
        sems[s] = sem_stack.enter_context(nc.semaphore(s))

    def emit_program(ctx: Ctx):
        S = lambda n: A[n]

        ld_identb = ctx.dma("gpsimd", S("identb")[:, :], ident, "wg")
        ld_mask = ctx.dma("sync", S("maskb")[:, :], maskT, "p7")
        ctx.dma("sync", S("flags")[:, 0:1], cflag, "p7")
        ld_flags = ctx.dma("sync", S("flags")[:, 1:2], uflag, "p7")
        ld_nw = ctx.dma("sync", S("nw")[:, 0:1], norw[:, None], "p7")
        ld_b2 = ctx.dma("sync", S("bq2")[:, 0:1], fq2b[:, None], "p7")
        ld_b4 = ctx.dma("sync", S("bk2")[:, 0:1], fk2b[:, None], "p7")
        ld_b1 = ctx.dma("gpsimd", S("b1row")[0:1, :], fq1b[None, :], "wg")
        ld_b3 = ctx.dma("gpsimd", S("b3row")[0:1, :], fk1b[None, :], "wg")
        ld_brow = ctx.dma("gpsimd", S("brow")[0:1, :], fbias[None, :], "wg")
        ctx.op("gpsimd", "memset", S("ones1f")[0:1, :], 1.0)
        ctx.op("gpsimd", "memset", S("ones1b")[0:1, :], 1.0)
        ctx.op("gpsimd", "memset", S("ones5b")[0:1, :], 1.0)
        gp_pre = ctx.op("gpsimd", "memset", S("epsb")[:, 0:1], EPS, inc="gp")

        ldx = [ctx.dma("scalar", S("xs0")[:, :], x_tm[0:128, :], "p0"),
               ctx.dma("scalar", S("xs1")[:, :], x_tm[128:256, :], "p1")]
        t_sq = [0, 0]
        t_rd = [0, 0]
        for i in range(NT128):
            par = "p%d" % (i % 2)
            xs = S("xs%d" % (i % 2))
            sq = S("sqj%d" % (i % 2))
            ctx.wait("scalar", par, ldx[i])
            ctx.wait("scalar", "dv", t_rd[i % 2])
            t_sq[i % 2] = ctx.op("scalar", "activation", sq[:, :], xs[:, :],
                                 AF.Square, inc="ac")
            ctx.wait("vector", par, ldx[i])
            ctx.op("vector", "tensor_reduce", S("scol")[:, i:i + 1],
                   xs[:, :], AX.X, ALU.add)
            ctx.wait("vector", "ac", t_sq[i % 2])
            t_rd[i % 2] = ctx.op("vector", "tensor_reduce",
                                 S("sscol")[:, i:i + 1], sq[:, :],
                                 AX.X, ALU.add, inc="dv")
            if i + 2 < NT128:
                ctx.wait("scalar", "dv", t_rd[i % 2])
                ldx.append(ctx.dma("scalar", xs[:, :],
                                   x_tm[128 * (i + 2):128 * (i + 3), :], par))
        ctx.wait("vector", "dv", t_rd[(NT128 - 1) % 2])
        ms = ctx.op("vector", "tensor_scalar_mul", S("mrst")[:, 0:16],
                    S("scol")[:, :], 1.0 / D, inc="dv")
        ctx.wait("vector", "dv", ms)
        ms = ctx.op("vector", "tensor_scalar_mul", S("vtmp")[:, :],
                    S("sscol")[:, :], 1.0 / D, inc="dv")
        ctx.wait("vector", "dv", ms)
        ms = ctx.op("vector", "tensor_tensor", S("mtmp")[:, :],
                    S("mrst")[:, 0:16], S("mrst")[:, 0:16], ALU.mult, inc="dv")
        ctx.wait("vector", "dv", ms)
        dv_var = ctx.op("vector", "tensor_tensor", S("vtmp")[:, :],
                        S("vtmp")[:, :], S("mtmp")[:, :], ALU.subtract,
                        inc="dv")
        ctx.wait("scalar", "dv", dv_var)
        ctx.wait("scalar", "gp", gp_pre)
        ms = ctx.op("scalar", "activation", S("mtmp")[:, :], S("vtmp")[:, :],
                    AF.Ln, bias=S("epsb")[:, 0:1], inc="ac")
        ctx.wait("scalar", "ac", ms)
        ac_std = ctx.op("scalar", "activation", S("mrst")[:, 16:32],
                        S("mtmp")[:, :], AF.Exp, scale=-0.5, inc="ac")
        ctx.wait("vector", "ac", ac_std)
        ms = ctx.op("vector", "tensor_copy", S("tmp64")[0:64, :],
                    S("mrst")[64:128, :], inc="dv")
        ctx.wait("vector", "dv", ms)
        ctx.op("vector", "transpose", S("mrsb")[0:32, 0:32],
               S("mrst")[0:32, 0:32])
        ctx.op("vector", "transpose", S("mrsb")[0:32, 32:64],
               S("mrst")[32:64, 0:32])
        ctx.op("vector", "transpose", S("mrsb")[0:32, 64:96],
               S("tmp64")[0:32, 0:32])
        dv_mr = ctx.op("vector", "transpose", S("mrsb")[0:32, 96:128],
                       S("tmp64")[32:64, 0:32], inc="dv")
        ctx.wait("scalar", "dv", dv_mr)
        ctx.dma("scalar", mr_sp, S("mrsb")[0:32, :], "p2")
        ctx.barrier()

        ld_wq = ctx.dma("gpsimd", S("wq")[:, :, :],
                        wqT.rearrange("c p f -> p c f"), "wg")
        ld_f1 = ctx.dma("gpsimd", S("f1")[:, :], fq1T, "wg")
        ld_f2 = ctx.dma("gpsimd", S("f2")[:, :], fq2T, "wg")
        ctx.dma("sync", S("m_row")[0:1, :],
                mr_sp[0:16, :].rearrange("a b -> (a b)")[None, :], "p5")
        ld_r = ctx.dma("sync", S("r_row")[0:1, :],
                       mr_sp[16:32, :].rearrange("a b -> (a b)")[None, :], "p5")
        ldxf = ctx.dma("sync", S("xf")[:, :, :],
                       x_fm[0].rearrange("c p t -> p c t"), "p0")
        xn_st = [0, 0]
        qm_st = [0, 0]
        pe_tile = [0, 0]
        ac_evac = [0, 0]
        ac_q1 = [0, 0]
        dv_qm = [0, 0]
        for t in range(NT512):
            xn = S("xn%d" % (t % 2))
            qm = S("qm%d" % (t % 2))
            ctx.wait("tensor", "p5", ld_r)
            ctx.wait("tensor", "p0", ldxf)
            pe_mb = ctx.op("tensor", "matmul", A["Mb"][:, :],
                           S("ones1f")[0:1, :],
                           S("m_row")[0:1, 512 * t:512 * (t + 1)],
                           start=True, stop=True)
            pe_rb = ctx.op("tensor", "matmul", A["Rb"][:, :],
                           S("ones1f")[0:1, :],
                           S("r_row")[0:1, 512 * t:512 * (t + 1)],
                           start=True, stop=True, inc="pe")
            ctx.wait("vector", "pe", max(pe_rb, pe_tile[t % 2]))
            ctx.wait("vector", "p%d" % (1 + t % 2), xn_st[t % 2])
            for c in range(16):
                ms = ctx.op("vector", "tensor_tensor", S("t1")[:, :],
                            S("xf")[:, c, :], A["Mb"][:, :], ALU.subtract,
                            inc="dv")
                ctx.wait("vector", "dv", ms)
                dv_xn = ctx.op("vector", "tensor_tensor", xn[:, c, :],
                               S("t1")[:, :], A["Rb"][:, :], ALU.mult,
                               inc="dv" if c == 15 else None)
            ctx.wait("scalar", "dv", dv_xn)
            xn_st[t % 2] = ctx.dma("scalar",
                                   xn_sp[t].rearrange("c p t -> p c t"),
                                   xn[:, :, :], "p%d" % (1 + t % 2))
            if t + 1 < NT512:
                ctx.wait("sync", "dv", dv_xn)
                ldxf = ctx.dma("sync", S("xf")[:, :, :],
                               x_fm[t + 1].rearrange("c p t -> p c t"), "p0")
            ctx.wait("tensor", "wg", max(ld_wq, ld_f1, ld_f2))
            ctx.wait("tensor", "dv", dv_xn)
            ctx.wait("vector", "p7", ld_b2)
            ctx.wait("vector", "p%d" % (3 + t % 2), qm_st[t % 2])
            ac_ev = {}
            pe_q2d = {}
            for h in range(H + 1):
                if h < H:
                    p = h % 2
                    qp = A["qp%d" % p]
                    ctx.wait("tensor", "ac", ac_evac[p])
                    for kc in range(16):
                        ctx.op("tensor", "matmul", qp[:, :],
                               S("wq")[:, kc, 128 * h:128 * (h + 1)],
                               xn[:, kc, :],
                               start=(kc == 0), stop=(kc == 15))
                    pe_q = ctx.mark("tensor", "pe")
                    ctx.wait("scalar", "pe", pe_q)
                    ctx.wait("scalar", "dv", dv_qm[p])
                    ac_evac[p] = ctx.op("scalar", "copy", S("qb%d" % p)[:, :],
                                        qp[:, :], inc="ac")
                    ac_ev[h] = ac_evac[p]
                if h >= 1:
                    hp = h - 1
                    p = hp % 2
                    q1p, q2p = A["q1p%d" % p], A["q2p%d" % p]
                    qb, q1s = S("qb%d" % p), S("q1s%d" % p)
                    ctx.wait("tensor", "ac", max(ac_ev[hp], ac_q1[p]))
                    ctx.wait("tensor", "dv", dv_qm[p])
                    ctx.wait("tensor", "wg", ld_b1)
                    ctx.wait("tensor", "gp", gp_pre)
                    ctx.op("tensor", "matmul", q1p[:, :], S("f1")[:, :],
                           qb[:, :], start=True, stop=False)
                    ctx.op("tensor", "matmul", q1p[:, :], S("b1row")[0:1, :],
                           S("ones5b")[0:1, :], start=False, stop=True)
                    pe_q2 = ctx.op("tensor", "matmul", q2p[:, :], S("f2")[:, :],
                                   qb[:, :], start=True, stop=True, inc="pe")
                    pe_q2d[hp] = pe_q2
                    ctx.wait("scalar", "pe", pe_q2)
                    ac_q1[p] = ctx.op("scalar", "activation", q1s[:, :],
                                      q1p[:, :], AF.Copy, scale=SCALE,
                                      inc="ac")
                    ctx.wait("vector", "ac", ac_q1[p])
                    dv_qm[p] = ctx.op("vector", "scalar_tensor_tensor",
                                      qm[:, hp, :], q2p[:, :], S("bq2")[:, 0:1],
                                      q1s[:, :], ALU.add, ALU.mult, inc="dv")
            pe_tile[t % 2] = pe_q2
            ctx.wait("scalar", "dv", max(dv_qm[0], dv_qm[1]))
            qm_st[t % 2] = ctx.dma("scalar",
                                   qm_sp[t].rearrange("c p t -> p c t"),
                                   qm[:, :, :], "p%d" % (3 + t % 2))
        ctx.barrier()

        ld_wk = ctx.dma("gpsimd", S("wk")[:, :, :],
                        wkT.rearrange("c p f -> p c f"), "wg")
        ld_g1 = ctx.dma("gpsimd", S("g1")[:, :], fk1T, "wg")
        ld_g2 = ctx.dma("gpsimd", S("g2")[:, :], fk2T, "wg")
        ldxn = [ctx.dma("sync", S("kxn0")[:, :, :],
                        xn_sp[0].rearrange("c p t -> p c t"), "p0")]
        km_st = [0, 0]
        kmt_st = [0, 0]
        pe_kt_tile = [0, 0]
        pe_ktile = [0, 0]
        ac_evac = [0, 0]
        ac_k1 = [0, 0]
        dv_km = [0, 0]
        ac_kt = [0, 0]
        for t in range(NT512):
            xn = S("kxn%d" % (t % 2))
            km = S("km%d" % (t % 2))
            if t + 1 < NT512:
                ctx.wait("sync", "pe", pe_ktile[(t + 1) % 2])
                ldxn.append(ctx.dma("sync",
                                    S("kxn%d" % ((t + 1) % 2))[:, :, :],
                                    xn_sp[t + 1].rearrange("c p t -> p c t"),
                                    "p%d" % ((t + 1) % 2)))
            ctx.wait("tensor", "wg", max(ld_wk, ld_g1, ld_g2))
            ctx.wait("tensor", "p%d" % (t % 2), ldxn[t])
            ctx.wait("vector", "p7", ld_b4)
            ctx.wait("vector", "p%d" % (2 + t % 2), km_st[t % 2])
            ctx.wait("vector", "pe", pe_kt_tile[t % 2])
            ac_ev = {}
            for h in range(H + 1):
                if h < H:
                    p = h % 2
                    kp = A["kp%d" % p]
                    ctx.wait("tensor", "ac", ac_evac[p])
                    for kc in range(16):
                        ctx.op("tensor", "matmul", kp[:, :],
                               S("wk")[:, kc, 128 * h:128 * (h + 1)],
                               xn[:, kc, :],
                               start=(kc == 0), stop=(kc == 15))
                    pe_k = ctx.mark("tensor", "pe")
                    ctx.wait("scalar", "pe", pe_k)
                    ctx.wait("scalar", "dv", dv_km[p])
                    ac_evac[p] = ctx.op("scalar", "copy", S("kb%d" % p)[:, :],
                                        kp[:, :], inc="ac")
                    ac_ev[h] = ac_evac[p]
                if h >= 1:
                    hp = h - 1
                    p = hp % 2
                    k1p, k2p = A["k1p%d" % p], A["k2p%d" % p]
                    kb, k1s = S("kb%d" % p), S("k1s%d" % p)
                    ctx.wait("tensor", "ac", max(ac_ev[hp], ac_k1[p]))
                    ctx.wait("tensor", "dv", dv_km[p])
                    ctx.wait("tensor", "wg", ld_b3)
                    ctx.wait("tensor", "gp", gp_pre)
                    ctx.op("tensor", "matmul", k1p[:, :], S("g1")[:, :],
                           kb[:, :], start=True, stop=False)
                    ctx.op("tensor", "matmul", k1p[:, :], S("b3row")[0:1, :],
                           S("ones5b")[0:1, :], start=False, stop=True)
                    pe_k2 = ctx.op("tensor", "matmul", k2p[:, :], S("g2")[:, :],
                                   kb[:, :], start=True, stop=True, inc="pe")
                    ctx.wait("scalar", "pe", pe_k2)
                    ac_k1[p] = ctx.op("scalar", "copy", k1s[:, :], k1p[:, :],
                                      inc="ac")
                    ctx.wait("vector", "ac", ac_k1[p])
                    dv_km[p] = ctx.op("vector", "scalar_tensor_tensor",
                                      km[:, hp, :], k2p[:, :], S("bk2")[:, 0:1],
                                      k1s[:, :], ALU.add, ALU.mult, inc="dv")
            pe_ktile[t % 2] = pe_k2
            ctx.wait("scalar", "dv", max(dv_km[0], dv_km[1]))
            km_st[t % 2] = ctx.dma("scalar",
                                   km_sp[t].rearrange("c p t -> p c t"),
                                   km[:, :, :], "p%d" % (2 + t % 2))
            ctx.wait("tensor", "dv", max(dv_km[0], dv_km[1]))
            ctx.wait("tensor", "wg", ld_identb)
            for cl in range(8):
                kmt = S("kmt%d" % (cl % 2))
                ctx.wait("scalar", "p%d" % (4 + cl % 2), kmt_st[cl % 2])
                for half in range(2):
                    ktp = A["ktp%d" % half]
                    ctx.wait("tensor", "ac", ac_kt[half])
                    for hh in range(8):
                        h = 8 * half + hh
                        ctx.op("tensor", "transpose",
                               ktp[0:64, 128 * hh:128 * (hh + 1)],
                               km[:, h, 64 * cl:64 * (cl + 1)],
                               S("identb")[:, :])
                    pe_t = ctx.mark("tensor", "pe")
                    ctx.wait("scalar", "pe", pe_t)
                    ac_kt[half] = ctx.op(
                        "scalar", "copy",
                        kmt[0:64, 1024 * half:1024 * (half + 1)],
                        ktp[0:64, :], inc="ac")
                ctx.wait("scalar", "ac", ac_kt[1])
                kmt_st[cl % 2] = ctx.dma("scalar", kmt_sp[8 * t + cl],
                                         kmt[0:64, :], "p%d" % (4 + cl % 2))
            pe_kt_tile[t % 2] = pe_t
        ctx.barrier()

        ld_wv = ctx.dma("gpsimd", S("wv")[:, :, :],
                        wvT.rearrange("c p f -> p c f"), "wg")
        ldxn = [ctx.dma("sync", S("vxn0")[:, :, :],
                        xn_sp[0].rearrange("c p t -> p c t"), "p0")]
        v_st = [0, 0]
        pe_vtile = [0, 0]
        ac_v = [0, 0]
        for t in range(NT512):
            xn = S("vxn%d" % (t % 2))
            if t + 1 < NT512:
                ctx.wait("sync", "pe", pe_vtile[(t + 1) % 2])
                ldxn.append(ctx.dma("sync",
                                    S("vxn%d" % ((t + 1) % 2))[:, :, :],
                                    xn_sp[t + 1].rearrange("c p t -> p c t"),
                                    "p%d" % ((t + 1) % 2)))
            ctx.wait("tensor", "p%d" % (t % 2), ldxn[t])
            ctx.wait("tensor", "wg", ld_wv)
            for s in range(4):
                sp = s % 2
                vp = A["vp%d" % sp]
                vb = S("vb%d" % sp)
                ctx.wait("tensor", "ac", ac_v[sp])
                for kc in range(16):
                    for fb in range(4):
                        ctx.op("tensor", "matmul",
                               vp[:, 512 * fb:512 * (fb + 1)],
                               xn[:, kc, 128 * s:128 * (s + 1)],
                               S("wv")[:, kc, 512 * fb:512 * (fb + 1)],
                               start=(kc == 0), stop=(kc == 15))
                pe_v = ctx.mark("tensor", "pe")
                ctx.wait("scalar", "pe", pe_v)
                ctx.wait("scalar", "p%d" % (2 + sp), v_st[sp])
                ac_v[sp] = ctx.op("scalar", "copy", vb[:, :], vp[:, :],
                                  inc="ac")
                ctx.wait("scalar", "ac", ac_v[sp])
                ch = 8 * t + 2 * s
                ctx.dma("scalar", v_sp[ch], vb[0:64, :], "p%d" % (2 + sp))
                v_st[sp] = ctx.dma("scalar", v_sp[ch + 1], vb[64:128, :],
                                   "p%d" % (2 + sp))
            pe_vtile[t % 2] = pe_v
        ctx.barrier()

        ld_kt = [ctx.dma("sync", S("tkmt0")[:, :],
                         kmt_sp[0:2].rearrange("c p f -> (c p) f"), "p0")]
        ld_tv = [ctx.dma("sync", S("tv0")[:, :],
                         v_sp[0:2].rearrange("c p f -> (c p) f"), "p0")]
        pe_tc = [0, 0]
        NCP = NCH // 2
        for c in range(NCP):
            kmt = S("tkmt%d" % (c % 2))
            tv = S("tv%d" % (c % 2))
            if c + 1 < NCP:
                par = "p%d" % ((c + 1) % 2)
                ctx.wait("sync", "pe", pe_tc[(c + 1) % 2])
                ld_kt.append(ctx.dma(
                    "sync", S("tkmt%d" % ((c + 1) % 2))[:, :],
                    kmt_sp[2 * c + 2:2 * c + 4].rearrange("c p f -> (c p) f"),
                    par))
                ld_tv.append(ctx.dma(
                    "sync", S("tv%d" % ((c + 1) % 2))[:, :],
                    v_sp[2 * c + 2:2 * c + 4].rearrange("c p f -> (c p) f"),
                    par))
            ctx.wait("tensor", "p%d" % (c % 2), max(ld_kt[c], ld_tv[c]))
            for h in range(H):
                ctx.op("tensor", "matmul", A["Tp"][:, 128 * h:128 * (h + 1)],
                       kmt[:, 128 * h:128 * (h + 1)],
                       tv[:, 128 * h:128 * (h + 1)],
                       start=(c == 0 and h % 4 == 0),
                       stop=(c == NCP - 1 and h % 4 == 3))
            pe_tc[c % 2] = ctx.mark("tensor", "pe")
        pe_T = pe_tc[(NCP - 1) % 2]
        ctx.wait("scalar", "pe", pe_T)
        ac_T = ctx.op("scalar", "copy", S("ttsb")[:, :], A["Tp"][:, :],
                      inc="ac")
        ctx.wait("vector", "ac", ac_T)
        ctx.wait("vector", "p7", ld_flags)
        dv_T = ctx.op("vector", "tensor_scalar_mul", S("ttsb")[:, :],
                      S("ttsb")[:, :], S("flags")[:, 0:1], inc="dv")
        ctx.wait("scalar", "dv", dv_T)
        g_in = ctx.dma("scalar", cc_in[:, :], S("ttsb")[:, :], "wg")
        ctx.wait("gpsimd", "wg", g_in)
        cc_t = ctx.collective([cc_in.ap().opt()], [cc_out.ap().opt()], "cc")
        ctx.wait("scalar", "cc", cc_t)
        g_out = ctx.dma("scalar", S("s0f")[:, :], cc_out[:, :], "wg")
        ctx.wait("vector", "wg", g_out)
        dv_s0 = ctx.op("vector", "tensor_scalar_mul", S("s0b")[:, :],
                       S("s0f")[:, :], S("flags")[:, 1:2], inc="dv")
        for r in range(RANK):
            kw = {} if r == 0 else {"accum_op": ALU.add}
            fa_t = ctx.dma("gpsimd", S("facc2")[:, :, :],
                           factsl[r].rearrange("(c p) e -> p c e", p=128),
                           "fa", **kw)
            ctx.wait("gpsimd", "fa", fa_t)
        g_fc = ctx.dma("gpsimd", cc_f_in[:, :],
                       S("facc2")[:, :, :].rearrange("p c e -> p (c e)"), "p6")
        ctx.wait("gpsimd", "p6", g_fc)
        ccf = ctx.inc("cc2", 1)
        if ctx.eng == "gpsimd":
            ctx.e.collective_compute(
                "AllGather", ALU.bypass,
                replica_groups=[[0, 1, 2, 3, 4, 5, 6, 7]],
                ins=[cc_f_in.ap().opt()], outs=[cc_f_out.ap().opt()],
            ).then_inc(ctx.sems["cc2"])
        ld_wo = ctx.dma("gpsimd", S("wo")[:, :, :],
                        woT.rearrange("c p f -> p c f"), "wg")
        ctx.barrier(exclude=("fa", "wg", "p6"))

        ld_aq = [ctx.dma("sync", S("aqm0")[:, :, :],
                         qm_sp[0].rearrange("c p t -> p c t"), "p2")]
        ld_ak = [ctx.dma("sync", S("akm0")[:, :, :],
                         km_sp[0].rearrange("c p t -> p c t"), "p2")]
        ld_av = [ctx.dma("sync", S("av0")[:, :],
                         v_sp[0:2].rearrange("c p f -> (c p) f"), "p0")]
        ld_akt = [ctx.dma("sync", S("akt0")[:, :],
                          kmt_sp[0:2].rearrange("c p f -> (c p) f"), "p0")]
        ctx.wait("tensor", "dv", dv_s0)
        ctx.wait("tensor", "wg", ld_identb)
        for h in range(H):
            ctx.op("tensor", "matmul", A["St"][:, 128 * h:128 * (h + 1)],
                   S("identb")[:, :], S("s0b")[:, 128 * h:128 * (h + 1)],
                   start=(h % 4 == 0), stop=False)
        pe_init = ctx.mark("tensor", "pe")
        kv_mark = [pe_init] * 4
        dv_mask = {}
        dv_mask_last = 0
        ac_snap_sg = [0, 0, 0, 0]
        ac_oev = [0, 0]
        dv_on_gp = [0, 0]
        dv_on_sg = {}
        pe_of_pair = [0, 0]
        ac_ofm = 0
        st_of = [0, 0]
        NPAIR = NCH // 2

        def scores(jn):
            nonlocal dv_mask_last
            t5n, jln = jn // 4, jn % 4
            qmn = S("aqm%d" % (t5n % 2))
            kmn = S("akm%d" % (t5n % 2))
            tsn = slice(128 * jln, 128 * (jln + 1))
            sbf = S("sbf%d" % (jn % 2))
            for sg in range(4):
                ctx.wait("tensor", "dv", dv_mask_last)
                for hh in range(4):
                    h = 4 * sg + hh
                    ctx.op("tensor", "matmul",
                           A["spp"][:, 128 * hh:128 * (hh + 1)],
                           kmn[:, h, tsn], qmn[:, h, tsn],
                           start=(hh == 0), stop=(hh == 3))
                pe_sp = ctx.mark("tensor", "pe")
                ctx.wait("vector", "pe", pe_sp)
                ctx.wait("vector", "p7", ld_mask)
                dv_mask[(jn, sg)] = ctx.op(
                    "vector", "tensor_tensor",
                    sbf[:, 512 * sg:512 * (sg + 1)],
                    A["spp"][:, :], S("maskb")[:, :], ALU.mult, inc="dv")
                dv_mask_last = dv_mask[(jn, sg)]

        def transposes(jp):
            nonlocal ac_ofm, pe_of_pair, st_of
            par = jp % 2
            onb = S("onb%d" % par)
            ofm = S("ofm%d" % par)
            for sg in range(4):
                ctx.wait("tensor", "dv", dv_on_sg[(jp, sg)])
                ctx.wait("tensor", "ac", ac_ofm)
                for hh in range(4):
                    ctx.op("tensor", "transpose",
                           A["ofp"][:, 128 * hh:128 * (hh + 1)],
                           onb[:, 512 * sg + 128 * hh:512 * sg + 128 * hh + 128],
                           S("identb")[:, :])
                pe_of = ctx.mark("tensor", "pe")
                ctx.wait("scalar", "pe", pe_of)
                ctx.wait("scalar", "p%d" % (4 + par), st_of[par])
                ac_ofm = ctx.op("scalar", "copy",
                                ofm[:, 4 * sg:4 * (sg + 1), :],
                                A["ofp"][:, :]
                                .rearrange("p (h t) -> p h t", h=4),
                                inc="ac")
            pe_of_pair[par] = pe_of
            ctx.wait("scalar", "ac", ac_ofm)
            st_of[par] = ctx.dma("scalar", o_sp[jp], ofm[:, :, :],
                                 "p%d" % (4 + par))

        ctx.wait("tensor", "p2", ld_ak[0])
        scores(0)
        for j in range(NPAIR):
            t5, jl = j // 4, j % 4
            qm = S("aqm%d" % (t5 % 2))
            akt = S("akt%d" % (j % 2))
            av = S("av%d" % (j % 2))
            if jl == 0 and t5 + 1 < NT512:
                par = "p%d" % (2 + (t5 + 1) % 2)
                ctx.wait("sync", "pe", kv_mark[3])
                ld_aq.append(ctx.dma("sync",
                                     S("aqm%d" % ((t5 + 1) % 2))[:, :, :],
                                     qm_sp[t5 + 1].rearrange("c p t -> p c t"),
                                     par))
                ld_ak.append(ctx.dma("sync",
                                     S("akm%d" % ((t5 + 1) % 2))[:, :, :],
                                     km_sp[t5 + 1].rearrange("c p t -> p c t"),
                                     par))
            if j + 1 < NPAIR:
                par = "p%d" % ((j + 1) % 2)
                ctx.wait("sync", "pe", kv_mark[3])
                ld_akt.append(ctx.dma(
                    "sync", S("akt%d" % ((j + 1) % 2))[:, :],
                    kmt_sp[2 * j + 2:2 * j + 4].rearrange("c p f -> (c p) f"),
                    par))
                ld_av.append(ctx.dma(
                    "sync", S("av%d" % ((j + 1) % 2))[:, :],
                    v_sp[2 * j + 2:2 * j + 4].rearrange("c p f -> (c p) f"),
                    par))
            tsl = slice(128 * jl, 128 * (jl + 1))
            ctx.wait("tensor", "p%d" % (2 + t5 % 2), ld_ak[t5])
            ctx.wait("tensor", "p%d" % (j % 2), max(ld_av[j], ld_akt[j]))
            sbf = S("sbf%d" % (j % 2))
            for sg in range(4):
                ctx.wait("scalar", "pe", kv_mark[sg])
                ac_snap_sg[sg] = ctx.op(
                    "scalar", "copy", S("snap")[:, 512 * sg:512 * (sg + 1)],
                    A["St"][:, 512 * sg:512 * (sg + 1)], inc="ac")
            for sg in range(4):
                gp_ = (4 * j + sg) % 2
                op = A["op%d" % gp_]
                obf = S("obf%d" % gp_)
                ctx.wait("tensor", "ac", max(ac_snap_sg[sg], ac_oev[gp_]))
                for hh in range(4):
                    h = 4 * sg + hh
                    ctx.op("tensor", "matmul",
                           op[:, 128 * hh:128 * (hh + 1)],
                           qm[:, h, tsl],
                           S("snap")[:, 128 * h:128 * (h + 1)],
                           start=(hh == 0), stop=False)
                ctx.wait("tensor", "dv", dv_mask[(j, sg)])
                for hh in range(4):
                    h = 4 * sg + hh
                    ctx.op("tensor", "matmul",
                           op[:, 128 * hh:128 * (hh + 1)],
                           sbf[:, 512 * sg + 128 * hh:512 * sg + 128 * hh + 128],
                           av[:, 128 * h:128 * (h + 1)],
                           start=False, stop=(hh == 3))
                pe_o = ctx.mark("tensor", "pe")
                for hh in range(4):
                    h = 4 * sg + hh
                    ctx.op("tensor", "matmul",
                           A["St"][:, 128 * h:128 * (h + 1)],
                           akt[:, 128 * h:128 * (h + 1)],
                           av[:, 128 * h:128 * (h + 1)],
                           start=False,
                           stop=(j == NPAIR - 1 and hh == 3))
                kv_mark[sg] = ctx.mark("tensor", "pe")
                ctx.wait("scalar", "pe", pe_o)
                ctx.wait("scalar", "dv", dv_on_gp[gp_])
                ac_oev[gp_] = ctx.op("scalar", "copy", obf[:, :], op[:, :],
                                     inc="ac")
                ctx.wait("vector", "ac", ac_oev[gp_])
                ms = ctx.op("vector", "tensor_tensor", S("osq")[:, :],
                            obf[:, :], obf[:, :], ALU.mult, inc="dv")
                ctx.wait("vector", "dv", ms)
                ctx.op("vector", "tensor_reduce",
                       S("ss8")[:, :].unsqueeze(2),
                       S("osq")[:, :].rearrange("p (g e) -> p g e", g=4),
                       AX.X, ALU.add)
                dv_ss = ctx.mark("vector", "dv")
                ctx.wait("scalar", "dv", dv_ss)
                ms = ctx.op("scalar", "activation", S("sd8")[:, :],
                            S("ss8")[:, :], AF.Ln, scale=1.0 / DK,
                            bias=S("epsb")[:, 0:1], inc="ac")
                ctx.wait("scalar", "ac", ms)
                ac_sd = ctx.op("scalar", "activation", S("rr8")[:, :],
                               S("sd8")[:, :], AF.Exp, scale=-0.5, inc="ac")
                ctx.wait("vector", "ac", ac_sd)
                ctx.wait("vector", "pe", pe_of_pair[j % 2])
                dv_on = ctx.op("vector", "tensor_tensor",
                               S("onb%d" % (j % 2))
                               [:, 512 * sg:512 * (sg + 1)]
                               .rearrange("p (g e) -> p g e", g=4),
                               obf[:, :].rearrange("p (g e) -> p g e", g=4),
                               S("rr8")[:, :].unsqueeze(2)
                               .broadcast_to([128, 4, DK]),
                               ALU.mult, inc="dv")
                dv_on_gp[gp_] = dv_on
                dv_on_sg[(j, sg)] = dv_on
            if j + 1 < NPAIR:
                ctx.wait("tensor", "p%d" % (2 + ((j + 1) // 4) % 2),
                         ld_ak[(j + 1) // 4])
                scores(j + 1)
            if j >= 1:
                transposes(j - 1)
        transposes(NPAIR - 1)
        ctx.wait("vector", "wg", ld_wo)
        ctx.wait("vector", "p7", ld_nw)
        dv_wo = ctx.op("vector", "tensor_scalar_mul",
                       S("wo")[:, :, :].rearrange("p c f -> p (c f)"),
                       S("wo")[:, :, :].rearrange("p c f -> p (c f)"),
                       S("nw")[:, 0:1], inc="dv")
        ctx.wait("scalar", "pe", max(kv_mark))
        ms = ctx.op("scalar", "copy", S("snap")[:, :], A["St"][:, :], inc="ac")
        ctx.wait("scalar", "ac", ms)
        ctx.dma("scalar", dbg5, S("snap")[:, :], "p6")
        ctx.barrier(exclude=("fa",))

        ccv = cc_f_out.ap().rearrange("(r p) (c e) -> r p c e", r=8, c=16)
        ctx.wait("sync", "cc2", ccf)
        for r in range(8):
            ld_F = ctx.dma("sync", S("Fb")[:, :, 256 * r:256 * (r + 1)],
                           ccv[r], "p7")
        ld_o = [ctx.dma("sync", S("osb0")[:, :, :], o_sp[0], "p0")]
        ld_xt = [ctx.dma("scalar", S("xt0")[:, :], x_tm[0:128, :], "p2")]
        ld_ox = [ctx.dma("sync", S("oxn")[:, :, :],
                         xn_sp[0].rearrange("c p t -> p c t"), "p6")]
        dv_out = [0, 0]
        st_out = [0, 0]
        dv_psum = [0, 0]
        pe_otile = [0, 0]
        for t in range(NT128):
            t5, s = t // 4, t % 4
            osb = S("osb%d" % (t % 2))
            xt = S("xt%d" % (t % 2))
            if t + 1 < NT128:
                ctx.wait("sync", "pe", pe_otile[(t + 1) % 2])
                ld_o.append(ctx.dma("sync",
                                    S("osb%d" % ((t + 1) % 2))[:, :, :],
                                    o_sp[t + 1], "p%d" % ((t + 1) % 2)))
                ctx.wait("scalar", "dv", dv_out[(t + 1) % 2])
                ld_xt.append(ctx.dma("scalar",
                                     S("xt%d" % ((t + 1) % 2))[:, :],
                                     x_tm[128 * (t + 1):128 * (t + 2), :],
                                     "p%d" % (2 + (t + 1) % 2)))
            ctx.wait("tensor", "p%d" % (t % 2), ld_o[t])
            ctx.wait("tensor", "p6", ld_ox[t5])
            ctx.wait("tensor", "p7", ld_F)
            ctx.wait("tensor", "dv", dv_wo)
            ctx.wait("tensor", "wg", ld_wo)
            ctx.wait("vector", "p%d" % (2 + t % 2), ld_xt[t])
            ctx.wait("vector", "p%d" % (4 + t % 2), st_out[t % 2])
            for fbp in range(2):
                pp = (2 * t + fbp) % 2
                ap, fp = A["ap%d" % pp], A["fp%d" % pp]
                ctx.wait("tensor", "dv", dv_psum[pp])
                for fb2 in range(2):
                    fb = 2 * fbp + fb2
                    fsl = slice(512 * fb, 512 * (fb + 1))
                    psl = slice(512 * fb2, 512 * (fb2 + 1))
                    for h in range(H):
                        ctx.op("tensor", "matmul", ap[:, psl],
                               osb[:, h, :], S("wo")[:, h, fsl],
                               start=(h == 0), stop=False)
                    ctx.op("tensor", "matmul", ap[:, psl],
                           S("ones1b")[0:1, :], S("brow")[0:1, fsl],
                           start=False, stop=True)
                    for kc in range(16):
                        ctx.op("tensor", "matmul", fp[:, psl],
                               S("oxn")[:, kc, 128 * s:128 * (s + 1)],
                               S("Fb")[:, kc, fsl],
                               start=(kc == 0), stop=(kc == 15))
                pe_fb = ctx.mark("tensor", "pe")
                ctx.wait("vector", "pe", pe_fb)
                for fb2 in range(2):
                    fb = 2 * fbp + fb2
                    fsl = slice(512 * fb, 512 * (fb + 1))
                    psl = slice(512 * fb2, 512 * (fb2 + 1))
                    ms = ctx.op("vector", "scalar_tensor_tensor", S("tf")[:, :],
                                fp[:, psl], FSC, xt[:, fsl],
                                ALU.mult, ALU.add, inc="dv")
                    ctx.wait("vector", "dv", ms)
                    dv_last = ctx.op("vector", "tensor_tensor",
                                     S("ot%d" % (t % 2))[:, fsl],
                                     S("tf")[:, :], ap[:, psl], ALU.add,
                                     inc="dv")
                dv_psum[pp] = dv_last
            pe_otile[t % 2] = pe_fb
            dv_out[t % 2] = dv_last
            if s == 3 and t5 + 1 < NT512:
                ctx.wait("sync", "pe", pe_fb)
                ld_ox.append(ctx.dma("sync", S("oxn")[:, :, :],
                                     xn_sp[t5 + 1].rearrange("c p t -> p c t"),
                                     "p6"))
            ctx.wait("scalar", "dv", dv_out[t % 2])
            st_out[t % 2] = ctx.dma("scalar", out[128 * t:128 * (t + 1), :],
                                    S("ot%d" % (t % 2))[:, :],
                                    "p%d" % (4 + t % 2))
        ctx.barrier()

    with nc.Block() as block:
        for eng in ENGS:
            def body(e, eng=eng):
                emit_program(Ctx(nc, sems, eng, e))
            getattr(block, eng)(body)
    sem_stack.close()
    persist.close()
    return nc


def _layout_inputs(inputs):
    x = np.ascontiguousarray(np.asarray(inputs["L0_attn_x"], dtype=np.float32))
    xr = x.reshape(16384, D)
    shared = {
        "wqT": np.ascontiguousarray(np.asarray(inputs["L0_attn_wq"]).T).reshape(16, 128, D),
        "wkT": np.ascontiguousarray(np.asarray(inputs["L0_attn_wk"]).T).reshape(16, 128, D),
        "wvT": np.ascontiguousarray(np.asarray(inputs["L0_attn_wv"]).T).reshape(16, 128, D),
        "woT": np.ascontiguousarray(np.asarray(inputs["L0_attn_wo"]).T).reshape(16, 128, D),
        "fq1T": np.ascontiguousarray(np.asarray(inputs["L0_attn_fq1_w"]).T),
        "fq2T": np.ascontiguousarray(np.asarray(inputs["L0_attn_fq2_w"]).T),
        "fk1T": np.ascontiguousarray(np.asarray(inputs["L0_attn_fk1_w"]).T),
        "fk2T": np.ascontiguousarray(np.asarray(inputs["L0_attn_fk2_w"]).T),
        "fq1b": np.asarray(inputs["L0_attn_fq1_b"], dtype=np.float32),
        "fq2b": np.asarray(inputs["L0_attn_fq2_b"], dtype=np.float32),
        "fk1b": np.asarray(inputs["L0_attn_fk1_b"], dtype=np.float32),
        "fk2b": np.asarray(inputs["L0_attn_fk2_b"], dtype=np.float32),
        "norw": np.asarray(inputs["L0_attn_norm_w"], dtype=np.float32),

        "fbias": np.asarray(inputs["L0_ffn_bias"], dtype=np.float32),
        "maskT": np.tile(np.block(
            [[np.triu(np.ones((64, 64), np.float32)), np.ones((64, 64), np.float32)],
             [np.zeros((64, 64), np.float32), np.triu(np.ones((64, 64), np.float32))]]),
            (1, 4)),
        "ident": np.eye(128, dtype=np.float32),
    }
    fact = np.asarray(inputs["L0_ffn_factors"], dtype=np.float32)
    in_maps = []
    for c in range(8):
        x_tm = np.ascontiguousarray(xr[2048 * c:2048 * (c + 1)])
        m = dict(shared)
        m["factsl"] = np.ascontiguousarray(fact[:, :, 256 * c:256 * (c + 1)])
        m["x_tm"] = x_tm
        m["x_fm"] = np.ascontiguousarray(
            x_tm.reshape(4, 512, 16, 128).transpose(0, 2, 3, 1))
        m["cflag"] = np.full((128, 1), 1.0 if c % 2 == 0 else 0.0, np.float32)
        m["uflag"] = np.full((128, 1), 0.0 if c % 2 == 0 else 1.0, np.float32)
        in_maps.append(m)
    return in_maps


def kernel(**inputs):
    in_maps = _layout_inputs(inputs)
    nc = build_program()
    res = run_bass_kernel_spmd(nc, in_maps, core_ids=list(range(8)))
    out = np.empty((4, 4096, D), np.float32)
    for c in range(8):
        out[c // 2, 2048 * (c % 2):2048 * (c % 2 + 1), :] = res.results[c]["out"]
    return out
